# version 43
# baseline (speedup 1.0000x reference)
import numpy as np
from contextlib import ExitStack

import concourse.bass as bass
import concourse.mybir as mybir
import concourse.bass_utils as bass_utils
import ml_dtypes

F32 = mybir.dt.float32
BF16 = mybir.dt.bfloat16
AF = mybir.ActivationFunctionType
ALU = mybir.AluOpType
AX = mybir.AxisListType.X

H = 1024
S = 4096
B = 4
NT = 16
TCORE = NT * 128
NKT = 32
LAT = 512
E = 8

ENGS = ("tensor", "vector", "scalar", "sync", "gpsimd")
DMA_Q = {"sync": "Dsy", "gpsimd": "Dgp", "scalar": "Dsc"}
QUEUES = ("Dsy", "Dgp", "Dsc")
UNSAFE = ("tt", "reduce", "recip", "accum")


NSLOT = 12


class Plan:

    def __init__(self):
        self.ops = {e: [] for e in ENGS}
        self.ncomp = {e: 0 for e in ENGS}
        self.ndma = {q: 0 for q in QUEUES}
        self.uses = {q: [0] * NSLOT for q in QUEUES}
        self.refd = {e: set() for e in ENGS}
        self.last_write = {}
        self.readers = {}
        self.prev_op = {e: None for e in ENGS}
        self.phase = ""
        self.phase_marks = []

    def mark(self, name):
        self.phase = name
        self.phase_marks.append((name, self.ncomp["tensor"]))

    def emit(self, eng, fn, reads=(), writes=(), cls="safe", dma=False, tag=""):
        q = DMA_Q[eng] if dma else eng
        waits = set()

        deferred = eng in ("vector", "scalar", "gpsimd")

        def need_wait(tok):
            return tok[0] == "Q" or tok[0] != eng or deferred

        for r in reads:
            lw = self.last_write.get(r)
            if lw is not None and need_wait(lw[0]):
                waits.add(lw[0])
        for w in writes:
            lw = self.last_write.get(w)
            if lw is not None and need_wait(lw[0]):
                waits.add(lw[0])
            for rd in self.readers.get(w, ()):
                if need_wait(rd):
                    waits.add(rd)
        if dma:
            slot = self.ndma[q] % NSLOT
            self.ndma[q] += 1
            if self.uses[q][slot] > 0:
                waits.add(("Q", q, slot, self.uses[q][slot]))
            self.uses[q][slot] += 1
            tok = ("Q", q, slot, self.uses[q][slot])
            self.ops[eng].append((fn, waits, (q, slot), tag))
        else:
            idx = self.ncomp[eng]
            self.ncomp[eng] += 1
            tok = (q, idx)
            self.ops[eng].append((fn, waits, False, tag))
        for w in writes:
            self.last_write[w] = (tok, cls)
            self.readers[w] = []
        for r in reads:
            self.readers.setdefault(r, []).append(tok)
        return tok

    def wait_all(self, eng):
        toks = set()
        for e in ENGS:
            if e != eng and self.ncomp[e]:
                toks.add((e, self.ncomp[e] - 1))
        for q in QUEUES:
            for s in range(NSLOT):
                if self.uses[q][s]:
                    toks.add(("Q", q, s, self.uses[q][s]))
        self.ops[eng].append((None, toks, False, "wait_all"))

    def barrier(self):
        for e in ENGS:
            self.wait_all(e)


def realize(plan, block, sems):
    for e in ENGS:
        for (fn, waits, dma, tag) in plan.ops[e]:
            for t in waits:
                if t[0] != "Q":
                    plan.refd[t[0]].add(t[1])
    incmap = {}
    for e in ENGS:
        vals = {}
        c = 0
        for idx in sorted(plan.refd[e]):
            c += 1
            vals[idx] = c
        incmap[e] = vals

    def run_engine(eng_obj, eng_name):
        cidx = 0
        waited = {}
        for (fn, waits, dma, tag) in plan.ops[eng_name]:
            best = {}
            for t in waits:
                if t[0] == "Q":
                    key = ("Q", t[1], t[2])
                    v = 16 * t[3]
                else:
                    key = t[0]
                    v = incmap[t[0]].get(t[1])
                    assert v is not None, (eng_name, t, tag)
                if best.get(key, -1) < v:
                    best[key] = v
            for key in sorted(best, key=repr):
                v = best[key]
                if waited.get(key, -1) >= v:
                    continue
                eng_obj.wait_ge(sems[key], v)
                waited[key] = v
            if fn is None:
                continue
            instr = fn(eng_obj)
            if dma:
                q, slot = dma
                instr.then_inc(sems[("Q", q, slot)], 16)
            else:
                if cidx in incmap[eng_name]:
                    instr.then_inc(sems[eng_name], 1)
                cidx += 1

    block.tensor(lambda t: run_engine(t, "tensor"))
    block.vector(lambda v: run_engine(v, "vector"))
    block.scalar(lambda s: run_engine(s, "scalar"))
    block.sync(lambda sy: run_engine(sy, "sync"))
    block.gpsimd(lambda g: run_engine(g, "gpsimd"))


def build_program(stage="full", debug=()):
    nc = bass.Bass("TRN2", target_bir_lowering=False)

    xseq = nc.dram_tensor("xseq", [S, H], F32, kind="ExternalInput").ap()
    xq = nc.dram_tensor("xq", [TCORE, H], F32, kind="ExternalInput").ap()
    wqT = nc.dram_tensor("wqT", [128, 8 * H], F32, kind="ExternalInput").ap()
    wkT = nc.dram_tensor("wkT", [128, 8 * H], F32, kind="ExternalInput").ap()
    wvT = nc.dram_tensor("wvT", [128, 8 * H], F32, kind="ExternalInput").ap()
    fc1Td = nc.dram_tensor("fc1T", [128, 8 * LAT], F32, kind="ExternalInput").ap()
    gateTd = nc.dram_tensor("gateT", [128, 8 * E], F32, kind="ExternalInput").ap()
    fc2Td = nc.dram_tensor("fc2T", [128, 4 * H], F32, kind="ExternalInput").ap()
    upTd = nc.dram_tensor("upT", [8, 128, 4096], F32, kind="ExternalInput").ap()
    downTd = nc.dram_tensor("downT", [8, 128, 4096], F32, kind="ExternalInput").ap()
    w1Td = nc.dram_tensor("w1T", [128, 16384], F32, kind="ExternalInput").ap()
    w2Td = nc.dram_tensor("w2T", [128, 16384], F32, kind="ExternalInput").ap()
    ident_in = nc.dram_tensor("ident", [128, 128], BF16, kind="ExternalInput").ap()
    masks_in = nc.dram_tensor("masks", [2, 128, 512], BF16, kind="ExternalInput").ap()
    cb_in = nc.dram_tensor("cb128", [128, 128], F32, kind="ExternalInput").ap()

    y = nc.dram_tensor("y", [TCORE, H], F32, kind="ExternalOutput").ap()
    x1s = nc.dram_tensor("x1s", [TCORE, H], F32, kind="Internal").ap()
    xfts = nc.dram_tensor("xfts", [NKT, 128, H], BF16, kind="Internal").ap()
    qtg_in = nc.dram_tensor("qtile_g", [1, NT], mybir.dt.int32,
                            kind="ExternalInput").ap()
    up_bs = nc.dram_tensor("up_bs", [8, 128, 4096], BF16, kind="Internal").ap()
    wq_bs = nc.dram_tensor("wq_bs", [128, 8192], BF16, kind="Internal").ap()
    dn_bs = nc.dram_tensor("dn_bs", [8, 128, 4096], BF16, kind="Internal").ap()
    qs = nc.dram_tensor("qs", [NT, 128, H], BF16, kind="Internal").ap()
    dbg = {}
    for name, shape, dt in (("dbg_k", [128, 8 * S], BF16),
                            ("dbg_v", [128, NKT * H], BF16),
                            ("dbg_q", [NT, 128, H], BF16),
                            ("dbg_lat", [128, 4 * TCORE], BF16),
                            ("dbg_wr", [128, 128], F32),
                            ("dbg_comb", [128, 4 * TCORE], BF16)):
        if name in debug:
            dbg[name] = nc.dram_tensor(name, shape, dt, kind="ExternalOutput").ap()

    P = Plan()

    with ExitStack() as es:
        sb = lambda name, shape, dt: es.enter_context(nc.sbuf_tensor(name, shape, dt))
        ident = sb("identt", [128, 128], BF16)
        masks = sb("maskst", [128, 2 * 512], BF16)
        stt_names = ("sx", "ssq", "mean", "vq", "m2", "var", "r0", "r2", "hv",
                     "wn", "rstd", "negmr")
        stA = {k: sb(f"st_{k}", [128, NT], F32) for k in stt_names}
        x_t = [sb(f"x_t{j}", [128, H], F32) for j in range(2)]
        xf_t = [sb(f"xf_t{j}", [128, H], BF16) for j in range(2)]
        xfTg = [sb(f"xfTg{j}", [128, 8 * 512], BF16) for j in range(2)]
        wkv_sb = sb("wkv_sb", [128, 16 * 1024], BF16)
        qtg_sb = sb("qtg_sb", [1, NT], mybir.dt.int32)
        x1_t = [sb(f"x1_t{j}", [128, H], F32) for j in range(2)]
        q_sb = [wkv_sb[:, j * 1024:(j + 1) * 1024] for j in range(2)]
        P_t = wkv_sb[:, 2048:2048 + 2048]
        PT_t = wkv_sb[:, 4096:4096 + 2048]

        psm = lambda name: es.enter_context(nc.psum_tensor(name, [128, 1024], F32))
        pA, pB, pC, pD = psm("pA"), psm("pB"), psm("pC"), psm("pD")
        PSN = {"pA": pA, "pB": pB, "pC": pC, "pD": pD}

        sems = {}
        for k in ENGS:
            sems[k] = es.enter_context(nc.semaphore(f"sem_{k}"))
        for q in QUEUES:
            for s in range(NSLOT):
                sems[("Q", q, s)] = es.enter_context(
                    nc.semaphore(f"sem_{q}_{s}"))

        evac_rr = [0]

        def evac(dst, srcp, reads, writes):
            evac_rr[0] ^= 1
            if evac_rr[0]:
                P.emit("vector", lambda e, dst=dst, srcp=srcp: e.tensor_copy(
                    dst, srcp), reads=reads, writes=writes)
            else:
                P.emit("scalar", lambda e, dst=dst, srcp=srcp: e.copy(
                    dst, srcp), reads=reads, writes=writes)

        P.emit("sync", lambda e: e.dma_start(ident[:], ident_in[:, :]),
               writes=["ident"], dma=True)
        P.emit("gpsimd", lambda e: e.dma_start(qtg_sb[:], qtg_in[:, :]),
               writes=["qtg"], dma=True)


        def load_wT(dst_sb, src_ap, ncols, key, nchunks, reads=()):
            P.emit("gpsimd", lambda e: e.dma_start(
                dst_sb[:, 0:nchunks * ncols], src_ap[:, :]),
                reads=reads, writes=[key], dma=True)



        def rstd_chain(c0, c1):
            c = lambda k: stA[k][:, c0:c1]
            rk = lambda *ks: [("st", k, j) for k in ks for j in range(c0, c1)]
            P.emit("vector", lambda e, c=c: e.tensor_scalar(
                c("mean"), c("sx"), 1.0 / H, None, ALU.mult),
                reads=rk("sx"), writes=rk("mean"))
            P.emit("vector", lambda e, c=c: e.tensor_scalar(
                c("vq"), c("ssq"), 1.0 / H, 1e-5, ALU.mult, ALU.add),
                reads=rk("ssq"), writes=rk("vq"))
            P.emit("vector", lambda e, c=c: e.scalar_tensor_tensor(
                c("m2"), c("mean"), 1.0, c("mean"), ALU.mult, ALU.mult),
                reads=rk("mean"), writes=rk("m2"))
            P.emit("vector", lambda e, c=c: e.tensor_tensor(
                c("var"), c("vq"), c("m2"), ALU.subtract),
                reads=rk("vq", "m2"), writes=rk("var"), cls="tt")
            P.emit("scalar", lambda e, c=c: e.activation(c("wn"), c("var"), AF.Sqrt),
                   reads=rk("var"), writes=rk("wn"))
            P.emit("vector", lambda e, c=c: e.reciprocal(c("r0"), c("wn")),
                   reads=rk("wn"), writes=rk("r0"), cls="recip")
            P.emit("vector", lambda e, c=c: e.scalar_tensor_tensor(
                c("r2"), c("r0"), 1.0, c("r0"), ALU.mult, ALU.mult),
                reads=rk("r0"), writes=rk("r2"))
            P.emit("vector", lambda e, c=c: e.scalar_tensor_tensor(
                c("hv"), c("r2"), -0.5, c("var"), ALU.mult, ALU.mult),
                reads=rk("r2", "var"), writes=rk("hv"))
            P.emit("vector", lambda e, c=c: e.tensor_scalar(
                c("wn"), c("hv"), 1.5, None, ALU.add),
                reads=rk("hv"), writes=rk("wn"))
            P.emit("vector", lambda e, c=c: e.scalar_tensor_tensor(
                c("rstd"), c("wn"), 1.0, c("r0"), ALU.mult, ALU.mult),
                reads=rk("wn", "r0"), writes=rk("rstd"))
            P.emit("vector", lambda e, c=c: e.scalar_tensor_tensor(
                c("negmr"), c("mean"), -1.0, c("rstd"), ALU.mult, ALU.mult),
                reads=rk("mean", "rstd"), writes=rk("negmr"))

        def normalize(xbuf, xfbuf, col, xkey, xfkey):
            P.emit("vector", lambda e: e.tensor_scalar(
                xf_t[xfbuf][:], x_t[xbuf][:], stA["rstd"][:, col:col + 1],
                stA["negmr"][:, col:col + 1], ALU.mult, ALU.add),
                reads=[xkey, ("st", "rstd", col), ("st", "negmr", col)],
                writes=[xfkey])


        def transpose_tile(xfbuf, xfkey, gslot, j, pt, pname):
            gk = ("xfTg", gslot)
            for half in range(2):
                bk = (pname, half)
                for cc in range(4):
                    hc = half * 4 + cc
                    P.emit("tensor", lambda e, hc=hc, half=half, cc=cc: e.matmul(
                        pt[:, half * 512 + cc * 128: half * 512 + (cc + 1) * 128],
                        xf_t[xfbuf][:, hc * 128:(hc + 1) * 128], ident[:],
                        start=True, stop=True),
                        reads=[xfkey, "ident"], writes=[bk])
            for half in range(2):
                bk = (pname, half)
                dst = xfTg[gslot][:].rearrange("p (hc f) -> p hc f", hc=8)[
                    :, half * 4:(half + 1) * 4, j * 128:(j + 1) * 128]
                srcp = pt[:, half * 512:(half + 1) * 512].rearrange(
                    "p (c f) -> p c f", c=4)
                evac(dst, srcp, [bk], [gk])

        wq_sb = wkv_sb[:, 0:8192]
        wk_sb = wkv_sb[:, 0:8192]
        wv_sb = wkv_sb[:, 8192:16384]
        K_fm = sb("K_fm", [128, 8 * S], BF16)
        V_km = sb("V_km", [128, NKT * H], BF16)

        def ln_stats2(xbuf, col, xkey):
            P.emit("vector", lambda e, xbuf=xbuf, col=col: e.scalar_tensor_tensor(
                masks[:], xpool[xbuf][:], 1.0, xpool[xbuf][:], ALU.mult, ALU.mult,
                accum_out=stA["ssq"][:, col:col + 1]),
                reads=[xkey], writes=["masks", ("st", "ssq", col)], cls="accum")
            P.emit("scalar", lambda e, xbuf=xbuf, col=col: e.activation(
                xpool[xbuf][:], xpool[xbuf][:], AF.Copy,
                accum_out=stA["sx"][:, col:col + 1]),
                reads=[xkey], writes=[xkey, ("st", "sx", col)], cls="accum")

        def normalize2(xbuf, xfbuf, col, xkey, xfkey):
            P.emit("vector", lambda e, xbuf=xbuf, xfbuf=xfbuf, col=col:
                   e.tensor_scalar(
                xf_t[xfbuf][:], xpool[xbuf][:], stA["rstd"][:, col:col + 1],
                stA["negmr"][:, col:col + 1], ALU.mult, ALU.add),
                reads=[xkey, ("st", "rstd", col), ("st", "negmr", col)],
                writes=[xfkey])

        xpool = x_t + x1_t

        def sweep(G, src_fn, proj_fn):
            def S1(g):
                for j in range(4):
                    t = g * 4 + j
                    xb = t % 4
                    ap, _ = src_fn(t)
                    if g > 0:
                        P.emit("sync", lambda e, ap=ap, xb=xb: e.dma_start(
                            xpool[xb][:], ap), writes=[("xp", xb)], dma=True)
                    ln_stats2(xb, (g % 4) * 4 + j, ("xp", xb))

            def S2(g):
                rstd_chain((g % 4) * 4, (g % 4) * 4 + 4)

            def S3(g):
                gslot = g % 2
                for j in range(4):
                    xb = (g * 4 + j) % 4
                    normalize2(xb, j % 2, (g % 4) * 4 + j, ("xp", xb),
                               ("xf_t", j % 2))
                    transpose_tile(j % 2, ("xf_t", j % 2), gslot, j, pC, "pC")

            S1(0)
            S2(0)
            for g in range(G):
                S3(g)
                if g + 1 < G:
                    S1(g + 1)
                    S2(g + 1)
                proj_fn(g, g % 2)

        ROT6 = [("pA", 0), ("pA", 1), ("pB", 0), ("pB", 1), ("pD", 0), ("pD", 1)]
        rotc = [0]

        def rot_bank():
            bk = ROT6[rotc[0] % 6]
            rotc[0] += 1
            return bk, PSN[bk[0]][:, bk[1] * 512:(bk[1] + 1) * 512]

        for t0_ in range(4):
            P.emit("sync", lambda e, t0_=t0_: e.dma_start(
                xpool[t0_][:], xseq[t0_ * 128:(t0_ + 1) * 128, :]),
                writes=[("xp", t0_)], dma=True)
        P.last_write["p1go"] = P.last_write[("xp", 3)]
        for hh in range(2):
            P.emit("gpsimd", lambda e, hh=hh: e.dma_start(
                wk_sb[:, hh * 4096:(hh + 1) * 4096],
                wkT[:, hh * 4096:(hh + 1) * 4096]),
                reads=["p1go"], writes=[("wk", hh)], dma=True)
        for hh in range(2):
            P.emit("gpsimd", lambda e, hh=hh: e.dma_start(
                wv_sb[:, hh * 4096:(hh + 1) * 4096],
                wvT[:, hh * 4096:(hh + 1) * 4096]),
                writes=[("wv", hh)], dma=True)
        def wq_precast():
            for hh in range(8):
                P.emit("gpsimd", lambda e, hh=hh: e.dma_start(
                    wq_bs[:, hh * 1024:(hh + 1) * 1024],
                    wqT[:, hh * 1024:(hh + 1) * 1024]),
                    writes=[("wqbs", hh)], dma=True)

        def proj_kv(kg, gslot):
            if kg == 2:
                wq_precast()
            gk = ("xfTg", gslot)
            for j in range(4):
                kt = kg * 4 + j
                srcp = xfTg[gslot][:].rearrange("p (hc f) -> p hc f", hc=8)[
                    :, :, j * 128:(j + 1) * 128]
                P.emit("scalar", lambda e, kt=kt, srcp=srcp: e.dma_start(
                    xfts[kt, :, :], srcp),
                    reads=[gk], writes=[("xfts", kt)], dma=True)
            if kg == 7:
                do_V(kg, gslot)
                do_K(kg, gslot)
            else:
                do_K(kg, gslot)
                do_V(kg, gslot)

        def do_K(kg, gslot):
            gk = ("xfTg", gslot)
            for fc in range(8):
                bk, pt = rot_bank()
                for hc in range(8):
                    P.emit("tensor", lambda e, fc=fc, hc=hc, pt=pt,
                           gslot=gslot: e.matmul(
                        pt, wk_sb[:, hc * H + fc * 128: hc * H + (fc + 1) * 128],
                        xfTg[gslot][:, hc * 512:(hc + 1) * 512],
                        start=(hc == 0), stop=(hc == 7)),
                        reads=[gk, ("wk", hc // 4)], writes=[bk])
                evac(K_fm[:, fc * S + kg * 512: fc * S + (kg + 1) * 512], pt,
                     [bk], [("K", kg)])

        def do_V(kg, gslot):
            gk = ("xfTg", gslot)
            for j in range(4):
                kc = kg * 4 + j
                for hf in range(2):
                    bk, pt = rot_bank()
                    for hc in range(8):
                        P.emit("tensor", lambda e, j=j, hf=hf, hc=hc, pt=pt,
                               gslot=gslot: e.matmul(
                            pt, xfTg[gslot][:, hc * 512 + j * 128:
                                            hc * 512 + (j + 1) * 128],
                            wv_sb[:, hc * H + hf * 512: hc * H + (hf + 1) * 512],
                            start=(hc == 0), stop=(hc == 7)),
                            reads=[gk, ("wv", hc // 4)], writes=[bk])
                    evac(V_km[:, kc * H + hf * 512: kc * H + (hf + 1) * 512],
                         pt, [bk], [("V", kc)])

        sweep(8, lambda t: (xseq[t * 128:(t + 1) * 128, :], None), proj_kv)
        if "dbg_k" in dbg:
            P.emit("sync", lambda e: e.dma_start(dbg["dbg_k"][:, :], K_fm[:]),
                   reads=[("K", kg2) for kg2 in range(8)], dma=True)
        if "dbg_v" in dbg:
            P.emit("sync", lambda e: e.dma_start(dbg["dbg_v"][:, :], V_km[:]),
                   reads=[("V", kc2) for kc2 in range(32)], dma=True)

        wq_sb2 = wkv_sb[:, 8192:16384]
        q_ev = wkv_sb[:, 2048:6144]

        QREGS = {}

        def qload(i, gslot):
            j = i % 4
            qb = i % 2

            def fn(e, i=i, qb=qb):
                if "r" not in QREGS:
                    QREGS["r"] = (e.alloc_register("qg_r"),
                                  e.alloc_register("qg_o"))
                gr, go = QREGS["r"]
                e.reg_load(gr, qtg_sb[:1, i:i + 1])
                e.reg_mul(go, gr, 128 * 1024)
                srcp = bass.AP(xfts.tensor, go, [[1024, 128], [1, 1024]])
                return e.dma_start(xf_t[qb][:], srcp)

            P.emit("gpsimd", fn,
                   reads=["qtg", ("xfts", 2 * i), ("xfts", 2 * i + 1)],
                   writes=[("xf_t", qb)], dma=True)
            dst = xfTg[gslot][:].rearrange("p (hc f) -> p hc f", hc=8)[
                :, :, j * 128:(j + 1) * 128]
            srcp2 = xf_t[qb][:].rearrange("p (hc f) -> p hc f", hc=8)
            P.emit("vector", lambda e, dst=dst, srcp2=srcp2: e.tensor_copy(
                dst, srcp2),
                reads=[("xf_t", qb)], writes=[("xfTg", gslot)])

        for j in range(4):
            qload(j, 0)
        for h2 in range(2):
            P.emit("sync", lambda e, h2=h2: e.dma_start(
                wq_sb2[:, h2 * 4096:(h2 + 1) * 4096],
                wq_bs[:, h2 * 4096:(h2 + 1) * 4096]),
                reads=[("wqbs", hh) for hh in range(4 * h2, 4 * h2 + 4)],
                writes=[("wq", h2), ("wv", h2)], dma=True)
        P.mark("P1q")

        def proj_q(qg, gslot):
            gk = ("xfTg", gslot)
            for fc in range(8):
                bk, pt = rot_bank()
                for hc in range(8):
                    P.emit("tensor", lambda e, fc=fc, hc=hc, pt=pt,
                           gslot=gslot: e.matmul(
                        pt, wq_sb2[:, hc * H + fc * 128: hc * H + (fc + 1) * 128],
                        xfTg[gslot][:, hc * 512:(hc + 1) * 512],
                        start=(hc == 0), stop=(hc == 7)),
                        reads=[gk, ("wq", hc // 4)], writes=[bk])
                dst = q_ev[:].rearrange("p (j f) -> p j f", j=4)[
                    :, :, fc * 128:(fc + 1) * 128]
                srcp = pt.rearrange("p (c q) -> p c q", c=4)
                P.emit("vector", lambda e, dst=dst, srcp=srcp: e.tensor_scalar(
                    dst, srcp, 1.0 / 32.0, None, ALU.mult),
                    reads=[bk], writes=["q_ev", ("wk", 0), ("wk", 1)])
            for j in range(4):
                i = qg * 4 + j
                P.emit("sync", lambda e, i=i, j=j: e.dma_start(
                    qs[i, :, :], q_ev[:, j * H:(j + 1) * H]),
                    reads=["q_ev"], writes=[("qs", i)], dma=True)

        for qg in range(4):
            if qg + 1 < 4:
                for j in range(4):
                    qload((qg + 1) * 4 + j, (qg + 1) % 2)
            proj_q(qg, qg % 2)
            if qg == 3:
                P.last_write["wq_dead"] = (P.readers[("wq", 1)][-1], "safe")
            if qg == 0:
                for j in range(2):
                    P.emit("gpsimd", lambda e, j=j: e.dma_start(
                        masks[:, j * 512:(j + 1) * 512], masks_in[j, :, :]),
                        writes=[("masks", j), "masks"], dma=True)
                for i in (0, 1):
                    p = i % 2
                    P.emit("sync", lambda e, i=i, p=p: e.dma_start(
                        q_sb[p][:], qs[i, :, :]),
                        reads=[("qs", i)], writes=[("q_sb", p), ("wk", 0)],
                        dma=True)
                    P.emit("scalar", lambda e, i=i, p=p: e.dma_start(
                        x_t[p][:], xq[i * 128:(i + 1) * 128, :]),
                        writes=[("x_t", p), ("xp", p)], dma=True)
        if "dbg_q" in dbg:
            for i in range(NT):
                P.emit("sync", lambda e, i=i: e.dma_start(
                    dbg["dbg_q"][i, :, :], qs[i, :, :]),
                    reads=[("qs", i)], dma=True)

        if stage == "p1":
            P.wait_all("sync")
            block = es.enter_context(nc.Block())
            realize(P, block, sems)
            return nc

        P.mark("P2")
        WBSK = lambda kind, sc: [("wbs", kind, sc, hh) for hh in range(4)]

        def precast_ud():
            for kind, srcd, dstd in (("u", upTd, up_bs), ("d", downTd, dn_bs)):
                for sc in range(8):
                    for hh in range(4):
                        P.emit("gpsimd", lambda e, sc=sc, hh=hh, srcd=srcd,
                               dstd=dstd: e.dma_start(
                            dstd[sc, :, hh * 1024:(hh + 1) * 1024],
                            srcd[sc, :, hh * 1024:(hh + 1) * 1024]),
                            writes=[("wbs", kind, sc, hh)], dma=True)

        nmkb = sb("nmkb", [128, 4], F32)
        nmv = [[sb(f"nm{p}{j}", [128, 1], F32) for j in range(2)]
               for p in range(2)]
        msc = sb("msc", [128, 8], F32)
        l_run = [sb(f"l_run{p}", [128, 1], F32) for p in range(2)]
        lparts = sb("lparts", [128, 4], F32)
        ls_t = [sb(f"ls_t{p}", [128, 1], F32) for p in range(2)]
        dtl = [sb(f"dtl{p}", [128, 1], F32) for p in range(2)]
        alpha = [sb(f"alpha{p}", [128, 1], F32) for p in range(2)]
        rl_t = [sb(f"rl_t{p}", [128, 1], F32) for p in range(2)]

        SBANKP = ((("pA", 0), ("pA", 1)), (("pB", 0), ("pB", 1)))
        SPS = (pA, pB)
        OBANK = (pD, pC)
        OKEY = ((("pD", 0), ("pD", 1)), (("pC", 0), ("pC", 1)))

        def p2_scores(i, sbi):
            p = i % 2
            L = i // 2 + 1
            kbs = min(2, L - 2 * sbi)
            for kb in range(kbs):
                ka = 2 * sbi + kb
                bk = SBANKP[p][kb]
                pt = SPS[p][:, bk[1] * 512:(bk[1] + 1) * 512]
                for fc in range(8):
                    P.emit("tensor", lambda e, fc=fc, ka=ka, pt=pt, p=p: e.matmul(
                        pt, q_sb[p][:, fc * 128:(fc + 1) * 128],
                        K_fm[:, fc * S + ka * 512: fc * S + (ka + 1) * 512],
                        start=(fc == 0), stop=(fc == 7)),
                        reads=[("q_sb", p), ("K", ka)], writes=[bk])

        def p2_soft(i, sbi):
            p = i % 2
            L = i // 2 + 1
            kbs = min(2, L - 2 * sbi)
            P.emit("vector", lambda e, p=p, kbs=kbs: e.reduce_max(
                nmkb[:, p * 2:p * 2 + 1], SPS[p][:, 0:kbs * 512], axis=AX,
                negate=True),
                reads=[SBANKP[p][0], SBANKP[p][1]], writes=[("nmkb", p, 0)],
                cls="reduce")
            items = [(nmkb[:, p * 2:p * 2 + 1], ("nmkb", p, 0))]
            if sbi > 0:
                items.append((nmv[p][(sbi - 1) % 2][:], ("nm", p, (sbi - 1) % 2)))
            nm_new = nmv[p][sbi % 2]
            nmk = ("nm", p, sbi % 2)
            if len(items) == 1:
                P.emit("vector", lambda e, a=items[0], nm_new=nm_new:
                       e.tensor_copy(nm_new[:], a[0]),
                       reads=[items[0][1]], writes=[nmk])
            elif len(items) == 2:
                P.emit("vector", lambda e, a=items[0], b_=items[1],
                       nm_new=nm_new: e.tensor_tensor(
                    nm_new[:], a[0], b_[0], ALU.min),
                    reads=[items[0][1], items[1][1]], writes=[nmk], cls="tt")
            else:
                P.emit("vector", lambda e, a=items[0], b_=items[1]: e.tensor_tensor(
                    msc[:, p:p + 1], a[0], b_[0], ALU.min),
                    reads=[items[0][1], items[1][1]], writes=[("msc", p)],
                    cls="tt")
                P.emit("vector", lambda e, c_=items[2], nm_new=nm_new, p=p:
                       e.tensor_tensor(
                    nm_new[:], msc[:, p:p + 1], c_[0], ALU.min),
                    reads=[("msc", p), items[2][1]], writes=[nmk], cls="tt")
            if sbi > 0:
                nm_old = nmv[p][(sbi - 1) % 2]
                P.emit("vector", lambda e, nm_new=nm_new, nm_old=nm_old, p=p:
                       e.tensor_tensor(dtl[p][:], nm_new[:], nm_old[:],
                                       ALU.subtract),
                       reads=[nmk, ("nm", p, (sbi - 1) % 2)], writes=[("dtl", p)],
                       cls="tt")
                P.emit("scalar", lambda e, p=p: e.activation(
                    alpha[p][:], dtl[p][:], AF.Exp),
                    reads=[("dtl", p)], writes=[("alpha", p)])
                P.emit("vector", lambda e, p=p: e.tensor_scalar_mul(
                    l_run[p][:], l_run[p][:], alpha[p][:]),
                    reads=[("l_run", p), ("alpha", p)], writes=[("l_run", p)])
                P.emit("scalar", lambda e, p=p: e.mul(
                    OBANK[p][:], OBANK[p][:], alpha[p][:]),
                    reads=[OKEY[p][0], OKEY[p][1], ("alpha", p)],
                    writes=[OKEY[p][0], OKEY[p][1]])
            for kb in range(kbs):
                ka = 2 * sbi + kb
                diag = (ka == L - 1)
                bk = SBANKP[p][kb]
                pt = SPS[p][:, bk[1] * 512:(bk[1] + 1) * 512]
                acc = None if diag else lparts[:, p * 2 + kb:p * 2 + kb + 1]
                pcol = p * 1024 + kb * 512
                P.emit("scalar", lambda e, pt=pt, acc=acc, nm_new=nm_new,
                       pcol=pcol: e.activation(
                    P_t[:, pcol:pcol + 512], pt, AF.Exp,
                    bias=nm_new[:], scale=1.0, accum_out=acc),
                    reads=[bk, nmk],
                    writes=[("P", p, kb), "q_ev"]
                    + ([] if diag else [("lp", p, kb)]),
                    cls="accum")
                if diag:
                    P.emit("vector", lambda e, pcol=pcol, p=p: e.tensor_tensor(
                        P_t[:, pcol:pcol + 512], P_t[:, pcol:pcol + 512],
                        masks[:, p * 512:(p + 1) * 512], ALU.mult),
                        reads=[("P", p, kb), ("masks", p)], writes=[("P", p, kb)],
                        cls="tt")
                    P.emit("vector", lambda e, pcol=pcol, p=p, kb=kb: e.reduce_sum(
                        lparts[:, p * 2 + kb:p * 2 + kb + 1],
                        P_t[:, pcol:pcol + 512], axis=AX),
                        reads=[("P", p, kb)], writes=[("lp", p, kb)], cls="reduce")
            if kbs > 1:
                P.emit("vector", lambda e, p=p: e.reduce_sum(
                    ls_t[p][:], lparts[:, p * 2:p * 2 + 2], axis=AX),
                    reads=[("lp", p, 0), ("lp", p, 1)], writes=[("ls", p)],
                    cls="reduce")
                ls_ap, ls_key = ls_t[p][:], ("ls", p)
            else:
                ls_ap, ls_key = lparts[:, p * 2:p * 2 + 1], ("lp", p, 0)
            if sbi == 0:
                P.emit("vector", lambda e, ls_ap=ls_ap, p=p: e.tensor_copy(
                    l_run[p][:], ls_ap), reads=[ls_key], writes=[("l_run", p)])
            else:
                P.emit("vector", lambda e, ls_ap=ls_ap, p=p: e.tensor_tensor(
                    l_run[p][:], l_run[p][:], ls_ap, ALU.add),
                    reads=[("l_run", p), ls_key], writes=[("l_run", p)], cls="tt")

        def p2_ptav(i, sbi):
            p = i % 2
            L = i // 2 + 1
            kbs = min(2, L - 2 * sbi)
            for kb in range(kbs):
                ka = 2 * sbi + kb
                bk = SBANKP[p][kb]
                pcol = p * 1024 + kb * 512
                for cc in range(4):
                    P.emit("tensor", lambda e, cc=cc, bk=bk, pcol=pcol, p=p:
                           e.matmul(
                        SPS[p][:, bk[1] * 512 + cc * 128:
                               bk[1] * 512 + (cc + 1) * 128],
                        P_t[:, pcol + cc * 128: pcol + (cc + 1) * 128],
                        ident[:], start=True, stop=True),
                        reads=[("P", p, kb), "ident"], writes=[bk])
                evac(PT_t[:, pcol:pcol + 512],
                     SPS[p][:, bk[1] * 512:(bk[1] + 1) * 512],
                     [bk], [("PT", p, kb), "q_ev"])
                for hf in range(2):
                    for cc in range(4):
                        kc = ka * 4 + cc
                        first = (sbi == 0 and kb == 0 and cc == 0)
                        last = (kb == kbs - 1 and cc == 3)
                        P.emit("tensor", lambda e, cc=cc, hf=hf, kc=kc, p=p,
                               pcol=pcol, first=first, last=last,
                               sg=(sbi > 0): e.matmul(
                            OBANK[p][:, hf * 512:(hf + 1) * 512],
                            PT_t[:, pcol + cc * 128: pcol + (cc + 1) * 128],
                            V_km[:, kc * H + hf * 512: kc * H + (hf + 1) * 512],
                            start=first, stop=last, skip_group_check=sg),
                            reads=[("PT", p, kb), ("V", kc)],
                            writes=[OKEY[p][hf]])

        def p2_fin(i):
            p = i % 2
            xb = i % 2
            P.emit("vector", lambda e, p=p: e.reciprocal(
                rl_t[p][:], l_run[p][:]),
                reads=[("l_run", p)], writes=[("rl", p)], cls="recip")
            P.emit("vector", lambda e, xb=xb, p=p: e.scalar_tensor_tensor(
                x1_t[p][:], OBANK[p][:], rl_t[p][:], x_t[xb][:], ALU.mult,
                ALU.add),
                reads=[OKEY[p][0], OKEY[p][1], ("rl", p), ("x_t", xb)],
                writes=[("x1_t", p)])
            P.emit("sync", lambda e, i=i, p=p: e.dma_start(
                x1s[i * 128:(i + 1) * 128, :], x1_t[p][:]),
                reads=[("x1_t", p)], writes=[("x1s", i)], dma=True)
            if stage == "full":
                def _stats(i=i, p=p):
                    P.emit("scalar", lambda e, i=i, p=p: e.activation(
                        x1_t[p][:], x1_t[p][:], AF.Copy,
                        accum_out=stA["sx"][:, i:i + 1]),
                        reads=[("x1_t", p)],
                        writes=[("x1_t", p), ("st", "sx", i)], cls="accum")
                    P.emit("scalar", lambda e, i=i, p=p: e.activation(
                        x1_t[p][:], x1_t[p][:], AF.Square,
                        accum_out=stA["ssq"][:, i:i + 1]),
                        reads=[("x1_t", p)],
                        writes=[("x1_t", p), ("st", "ssq", i)], cls="accum")
                pending_stats.append(_stats)
            if stage == "p2":
                P.emit("sync", lambda e, i=i, p=p: e.dma_start(
                    y[i * 128:(i + 1) * 128, :], x1_t[p][:]),
                    reads=[("x1_t", p)], dma=True)

        def p2_load_q(pr):
            for i in (2 * pr, 2 * pr + 1):
                p = i % 2
                P.emit("sync", lambda e, i=i, p=p: e.dma_start(
                    q_sb[p][:], qs[i, :, :]),
                    reads=[("qs", i)], writes=[("q_sb", p)], dma=True)

        def p2_load_x(pr):
            for i in (2 * pr, 2 * pr + 1):
                p = i % 2
                P.emit("sync", lambda e, i=i, p=p: e.dma_start(
                    x_t[p][:], xq[i * 128:(i + 1) * 128, :]),
                    writes=[("x_t", p)], dma=True)

        fc1_sb = wkv_sb[:, 8192:12288]
        gate_sb = masks[:, 0:64]
        fc2_sb = wkv_sb[:, 12288:16384]

        pending_stats = []
        for pr in range(8):
            i0, i1 = 2 * pr, 2 * pr + 1
            L = pr + 1
            nsb2 = (L + 1) // 2
            for k in range(nsb2):
                if k == 0 and pr == 0:
                    p2_scores(i0, 0)
                    p2_scores(i1, 0)
                if k == nsb2 - 1 and pr < 7:
                    p2_load_q(pr + 1)
                p2_soft(i0, k)
                p2_ptav(i0, k)
                if k + 1 < nsb2:
                    p2_scores(i0, k + 1)
                elif pr < 7:
                    p2_scores(2 * pr + 2, 0)
                p2_soft(i1, k)
                if k == 0:
                    for f in pending_stats:
                        f()
                    pending_stats.clear()
                p2_ptav(i1, k)
                if k + 1 < nsb2:
                    p2_scores(i1, k + 1)
                elif pr < 7:
                    p2_scores(2 * pr + 3, 0)
            p2_fin(i0)
            p2_fin(i1)
            if pr == 7:
                for f in pending_stats:
                    f()
                pending_stats.clear()
            if pr < 7:
                p2_load_x(pr + 1)
            if pr == 1:
                precast_ud()
            if pr == 6:
                rstd_chain(0, 12)
                load_wT(fc1_sb, fc1Td, 512, "fc1", 8, reads=["wq_dead"])
                load_wT(fc2_sb, fc2Td, 1024, "fc2", 4, reads=["wq_dead"])

        w1_sb = V_km[:, 0:16384]
        w2_sb = V_km[:, 16384:32768]
        P.emit("gpsimd", lambda e: e.dma_start(
            gate_sb[:, 0:64], gateTd[:, :]),
            writes=["gate", ("masks", 0)], dma=True)

        def w12_loads():
            gating = ["p3ago"]
            for qtr in range(4):
                P.emit("gpsimd", lambda e, qtr=qtr: e.dma_start(
                    w1_sb[:, qtr * 4096:(qtr + 1) * 4096],
                    w1Td[:, qtr * 4096:(qtr + 1) * 4096]),
                    reads=gating,
                    writes=[("w1", qtr)] + [("V", 4 * qtr + j)
                                            for j in range(4)], dma=True)
                P.emit("gpsimd", lambda e, qtr=qtr: e.dma_start(
                    w2_sb[:, qtr * 4096:(qtr + 1) * 4096],
                    w2Td[:, qtr * 4096:(qtr + 1) * 4096]),
                    reads=gating,
                    writes=[("w2", qtr)] + [("V", 16 + 4 * qtr + j)
                                            for j in range(4)], dma=True)

        if stage == "p2":
            P.wait_all("sync")
            block = es.enter_context(nc.Block())
            realize(P, block, sems)
            return nc

        P.mark("P3a")

        xfT = K_fm[:, 0:8 * 2048]
        latT = K_fm[:, 8 * 2048:8 * 2048 + 4 * 2048]
        comb = K_fm[:, 8 * 2048 + 4 * 2048:8 * 2048 + 8 * 2048]
        h_sh = V_km[:, 8 * 2048:8 * 2048 + 16 * 512 * 2]

        cb_sb = sb("cbt", [128, 128], F32)
        Rs = sb("Rs", [128, 128], F32)
        Rm = sb("Rm", [128, 128], F32)
        Rx = sb("Rx", [128, 128], F32)
        Ry = sb("Ry", [128, 128], F32)
        Rz = sb("Rz", [128, 128], F32)
        Rw = sb("Rw", [128, 128], F32)
        Rq = sb("Rq", [128, 128], F32)
        R64 = sb("R64", [128, 64], F32)
        R32 = sb("R32", [128, 32], F32)
        g16 = {k: sb(f"g16_{k}", [128, 16], F32) for k in
               ("m1a", "n1a", "m2a", "n2a", "Ma", "mma", "nna", "seca", "gA",
                "m1b", "n1b", "m2b", "n2b", "Mb", "mmb", "nnb", "secb", "gB",
                "d", "mA", "mB", "t1", "cnt", "t2c", "two", "d12", "t2",
                "sg", "need", "den", "rden")}
        t1b = [masks[:, 64 + j * 128:64 + (j + 1) * 128] for j in range(2)]
        r_bf = xfTg[0][:, 0:2048]
        rw_bf = [xfTg[0][:, 2048:4096], xfTg[1][:, 2048:4096]]
        r2w_bf = xfTg[1][:, 0:2048]
        r_sh = [xfTg[1][:, 2048 + j * 512:2048 + (j + 1) * 512] for j in range(2)]
        out_sb = x1_t

        P.emit("sync", lambda e: e.dma_start(cb_sb[:], cb_in[:, :]),
               writes=["cb"], dma=True)


        P.mark("P3b")
        ROT4F = [("pA", 0), ("pA", 1), ("pB", 0), ("pB", 1)]
        rotf = [0]
        def p3a_dma(i):
            xb = i % 4
            alias = ("x_t", xb) if xb < 2 else ("x1_t", xb - 2)
            P.emit("sync", lambda e, i=i, xb=xb: e.dma_start(
                xpool[xb][:], x1s[i * 128:(i + 1) * 128, :]),
                reads=[("x1s", i)], writes=[("xp", xb), alias], dma=True)

        def fc1_group(tg, s_evac=False):
            for Lc in range(4):
                bk = ROT4F[rotf[0] % 4]
                rotf[0] += 1
                pt = PSN[bk[0]][:, bk[1] * 512:(bk[1] + 1) * 512]
                for hc in range(8):
                    P.emit("tensor", lambda e, Lc=Lc, tg=tg, hc=hc, pt=pt:
                           e.matmul(
                        pt, fc1_sb[:, hc * 512 + Lc * 128:
                                   hc * 512 + (Lc + 1) * 128],
                        xfT[:, hc * 2048 + tg * 512:
                            hc * 2048 + (tg + 1) * 512],
                        start=(hc == 0), stop=(hc == 7)),
                        reads=[("xfT", tg * 4), ("xfT", tg * 4 + 1),
                               ("xfT", tg * 4 + 2), ("xfT", tg * 4 + 3),
                               "fc1"], writes=[bk])
                dst = latT[:, Lc * 2048 + tg * 512:Lc * 2048 + (tg + 1) * 512]
                if s_evac:
                    P.emit("scalar", lambda e, dst=dst, pt=pt: e.copy(dst, pt),
                           reads=[bk], writes=["lat"])
                else:
                    evac(dst, pt, [bk], ["lat"])

        p3a_dma(0)
        p3a_dma(1)
        for i in range(NT):
            xb = i % 4
            fb = i % 2
            if i + 2 < NT:
                p3a_dma(i + 2)
            if i == 2:
                P.last_write["p3ago"] = P.last_write[("xf_t", 1)]
                w12_loads()
            if i == 3:
                rstd_chain(12, 16)
            normalize2(xb, fb, i, ("xp", xb), ("xf_t", fb))
            for half in range(2):
                bk = ("pC", half)
                for cc in range(4):
                    hc = half * 4 + cc
                    P.emit("tensor", lambda e, hc=hc, half=half, cc=cc, fb=fb:
                           e.matmul(
                        pC[:, half * 512 + cc * 128: half * 512 + (cc + 1) * 128],
                        xf_t[fb][:, hc * 128:(hc + 1) * 128], ident[:],
                        start=True, stop=True),
                        reads=[("xf_t", fb), "ident"], writes=[bk])
            for half in range(2):
                bk = ("pC", half)
                dst = xfT.rearrange("p (hc f) -> p hc f", hc=8)[
                    :, half * 4:(half + 1) * 4, i * 128:(i + 1) * 128]
                srcp = pC[:, half * 512:(half + 1) * 512].rearrange(
                    "p (c f) -> p c f", c=4)
                evac(dst, srcp, [bk], [("xfT", i)])
            gh = i % 2
            for hc in range(8):
                P.emit("tensor", lambda e, hc=hc, i=i, gh=gh: e.matmul(
                    pD[:, gh * 512:gh * 512 + 8],
                    xfT[:, hc * 2048 + i * 128: hc * 2048 + (i + 1) * 128],
                    gate_sb[:, hc * 8:(hc + 1) * 8],
                    start=(hc == 0), stop=(hc == 7)),
                    reads=[("xfT", i), "gate"], writes=[("pD", gh)])
            P.emit("scalar", lambda e, i=i, gh=gh: e.activation(
                Rs[:, i:i + 113:16], pD[:, gh * 512:gh * 512 + 8], AF.Sigmoid),
                reads=[("pD", gh)], writes=["Rs"])
            if i % 4 == 3 and i < 8:
                fc1_group(i // 4)
        fc1_group(2, s_evac=True)
        fc1_group(3, s_evac=True)
        if "dbg_lat" in dbg:
            P.emit("sync", lambda e: e.dma_start(dbg["dbg_lat"][:, :], latT),
                   reads=["lat"], dma=True)

        P.mark("P3c")

        def tt(dst, dk, a, ak, b_, bk_, op):
            P.emit("vector", lambda e: e.tensor_tensor(dst, a, b_, op),
                   reads=list(ak) + list(bk_), writes=[dk], cls="tt")

        def bc8(t16):
            return bass.AP(t16, 0, [[16, 128], [0, 8], [1, 16]])

        r3 = lambda t: t[:].rearrange("p (e c) -> p e c", e=8)
        gk_ = lambda n: ("g", n)

        tt(Rm[:], "Rm", Rs[:], ["Rs"], cb_sb[:], ["cb"], ALU.add)
        for gix, base in (("a", 0), ("b", 64)):
            gg = lambda k: g16[k + gix]
            c0, c1, c2, c3 = (Rm[:, base + 16 * t: base + 16 * (t + 1)]
                              for t in range(4))
            tt(gg("m1")[:], gk_("m1" + gix), c0, ["Rm"], c1, ["Rm"], ALU.max)
            tt(gg("n1")[:], gk_("n1" + gix), c0, ["Rm"], c1, ["Rm"], ALU.min)
            tt(gg("m2")[:], gk_("m2" + gix), c2, ["Rm"], c3, ["Rm"], ALU.max)
            tt(gg("n2")[:], gk_("n2" + gix), c2, ["Rm"], c3, ["Rm"], ALU.min)
            tt(gg("M")[:], gk_("M" + gix), gg("m1")[:], [gk_("m1" + gix)],
               gg("m2")[:], [gk_("m2" + gix)], ALU.max)
            tt(gg("mm")[:], gk_("mm" + gix), gg("m1")[:], [gk_("m1" + gix)],
               gg("m2")[:], [gk_("m2" + gix)], ALU.min)
            tt(gg("nn")[:], gk_("nn" + gix), gg("n1")[:], [gk_("n1" + gix)],
               gg("n2")[:], [gk_("n2" + gix)], ALU.max)
            tt(gg("sec")[:], gk_("sec" + gix), gg("mm")[:], [gk_("mm" + gix)],
               gg("nn")[:], [gk_("nn" + gix)], ALU.max)
            tt(g16["gA" if gix == "a" else "gB"][:], gk_("g" + gix),
               gg("M")[:], [gk_("M" + gix)], gg("sec")[:], [gk_("sec" + gix)],
               ALU.add)
        tt(g16["d"][:], gk_("d"), g16["gA"][:], [gk_("ga")], g16["gB"][:],
           [gk_("gb")], ALU.subtract)
        P.emit("vector", lambda e: e.tensor_single_scalar(
            g16["mA"][:], g16["d"][:], 0.0, ALU.is_ge),
            reads=[gk_("d")], writes=[gk_("mA")])
        P.emit("vector", lambda e: e.tensor_single_scalar(
            g16["mB"][:], g16["d"][:], 0.0, ALU.is_lt),
            reads=[gk_("d")], writes=[gk_("mB")])
        for ee in range(8):
            mk_ = gk_("mA") if ee < 4 else gk_("mB")
            mt = g16["mA"] if ee < 4 else g16["mB"]
            tt(Rm[:, ee * 16:(ee + 1) * 16], "Rm",
               Rm[:, ee * 16:(ee + 1) * 16], ["Rm"], mt[:], [mk_], ALU.mult)
        tt(R64[:], "R64", Rm[:, 0:64], ["Rm"], Rm[:, 64:128], ["Rm"], ALU.max)
        tt(R32[:], "R32", R64[:, 0:32], ["R64"], R64[:, 32:64], ["R64"], ALU.max)
        tt(g16["t1"][:], gk_("t1"), R32[:, 0:16], ["R32"], R32[:, 16:32],
           ["R32"], ALU.max)
        tt(r3(Rx), "Rx", r3(Rm), ["Rm"], bc8(g16["t1"]), [gk_("t1")], ALU.is_equal)
        tt(R64[:], "R64", Rx[:, 0:64], ["Rx"], Rx[:, 64:128], ["Rx"], ALU.add)
        tt(R32[:], "R32", R64[:, 0:32], ["R64"], R64[:, 32:64], ["R64"], ALU.add)
        tt(g16["cnt"][:], gk_("cnt"), R32[:, 0:16], ["R32"], R32[:, 16:32],
           ["R32"], ALU.add)
        P.emit("vector", lambda e: e.tensor_single_scalar(
            Ry[:], Rx[:], -1e30, ALU.mult), reads=["Rx"], writes=["Ry"])
        tt(Ry[:], "Ry", Rm[:], ["Rm"], Ry[:], ["Ry"], ALU.add)
        tt(R64[:], "R64", Ry[:, 0:64], ["Ry"], Ry[:, 64:128], ["Ry"], ALU.max)
        tt(R32[:], "R32", R64[:, 0:32], ["R64"], R64[:, 32:64], ["R64"], ALU.max)
        tt(g16["t2c"][:], gk_("t2c"), R32[:, 0:16], ["R32"], R32[:, 16:32],
           ["R32"], ALU.max)
        P.emit("vector", lambda e: e.tensor_single_scalar(
            g16["two"][:], g16["cnt"][:], 2.0, ALU.is_ge),
            reads=[gk_("cnt")], writes=[gk_("two")])
        tt(g16["d12"][:], gk_("d12"), g16["t1"][:], [gk_("t1")],
           g16["t2c"][:], [gk_("t2c")], ALU.subtract)
        tt(g16["d12"][:], gk_("d12"), g16["d12"][:], [gk_("d12")],
           g16["two"][:], [gk_("two")], ALU.mult)
        tt(g16["t2"][:], gk_("t2"), g16["t2c"][:], [gk_("t2c")],
           g16["d12"][:], [gk_("d12")], ALU.add)
        tt(r3(Rx), "Rx", r3(Rm), ["Rm"], bc8(g16["t2"]), [gk_("t2")], ALU.is_gt)
        tt(R64[:], "R64", Rx[:, 0:64], ["Rx"], Rx[:, 64:128], ["Rx"], ALU.add)
        tt(R32[:], "R32", R64[:, 0:32], ["R64"], R64[:, 32:64], ["R64"], ALU.add)
        tt(g16["sg"][:], gk_("sg"), R32[:, 0:16], ["R32"], R32[:, 16:32],
           ["R32"], ALU.add)
        P.emit("vector", lambda e: e.tensor_scalar(
            g16["need"][:], g16["sg"][:], -1.0, 2.0, ALU.mult, ALU.add),
            reads=[gk_("sg")], writes=[gk_("need")])
        tt(r3(Ry), "Ry", r3(Rm), ["Rm"], bc8(g16["t2"]), [gk_("t2")], ALU.is_equal)
        P.emit("vector", lambda e: e.tensor_copy(Rz[:, 0:16], Ry[:, 0:16]),
               reads=["Ry"], writes=["Rz"])
        tt(Rz[:, 16:128], "Rz", Ry[:, 16:128], ["Ry"], Ry[:, 0:112], ["Ry"],
           ALU.add)
        P.emit("vector", lambda e: e.tensor_copy(Rx[:, 0:32], Rz[:, 0:32]),
               reads=["Rz"], writes=["Rx"])
        tt(Rx[:, 32:128], "Rx", Rz[:, 32:128], ["Rz"], Rz[:, 0:96], ["Rz"],
           ALU.add)
        P.emit("vector", lambda e: e.tensor_copy(Rz[:, 0:64], Rx[:, 0:64]),
               reads=["Rx"], writes=["Rz"])
        tt(Rz[:, 64:128], "Rz", Rx[:, 64:128], ["Rx"], Rx[:, 0:64], ["Rx"],
           ALU.add)
        tt(r3(Rx), "Rx", r3(Rz), ["Rz"], bc8(g16["need"]), [gk_("need")],
           ALU.is_le)
        tt(Ry[:], "Ry", Ry[:], ["Ry"], Rx[:], ["Rx"], ALU.mult)
        tt(r3(Rx), "Rx", r3(Rm), ["Rm"], bc8(g16["t2"]), [gk_("t2")], ALU.is_gt)
        tt(Ry[:], "Ry", Ry[:], ["Ry"], Rx[:], ["Rx"], ALU.add)
        tt(Rx[:], "Rx", Ry[:], ["Ry"], Rs[:], ["Rs"], ALU.mult)
        tt(R64[:], "R64", Rx[:, 0:64], ["Rx"], Rx[:, 64:128], ["Rx"], ALU.add)
        tt(R32[:], "R32", R64[:, 0:32], ["R64"], R64[:, 32:64], ["R64"], ALU.add)
        tt(g16["den"][:], gk_("den"), R32[:, 0:16], ["R32"], R32[:, 16:32],
           ["R32"], ALU.add)
        P.emit("vector", lambda e: e.tensor_scalar(
            g16["den"][:], g16["den"][:], 1.0, 1e-20, ALU.mult, ALU.add),
            reads=[gk_("den")], writes=[gk_("den")])
        P.emit("vector", lambda e: e.reciprocal(g16["rden"][:], g16["den"][:]),
               reads=[gk_("den")], writes=[gk_("rden")], cls="recip")
        tt(r3(Rw), "Rw", r3(Rx), ["Rx"], bc8(g16["rden"]), [gk_("rden")],
           ALU.mult)
        if "dbg_wr" in dbg:
            P.emit("sync", lambda e: e.dma_start(dbg["dbg_wr"][:, :], Rw[:]),
                   reads=["Rw"], dma=True)
        P.emit("scalar", lambda e: e.activation(Rq[:], Rw[:], AF.Sqrt),
               reads=["Rw"], writes=["Wq"])
        wbtb = [wkv_sb[:, 0:4096], wkv_sb[:, 4096:8192]]

        def build_wbt_steps(tg):
            wb = wbtb[tg % 2]
            for ee in range(8):
                def step(ee=ee, wb=wb, tg=tg):
                    for ii in range(4):
                        i = tg * 4 + ii
                        col = ee * 16 + i
                        tb = ii % 2
                        P.emit("vector", lambda e, col=col, tb=tb: e.tensor_copy(
                            t1b[tb], Rq[:, col:col + 1].broadcast_to([128, 128])),
                            reads=["Wq"], writes=[("t1b", tb)])
                        P.emit("tensor", lambda e, tb=tb, ii=ii, tg=tg: e.matmul(
                            pD[:, (tg % 2) * 512 + ii * 128:
                               (tg % 2) * 512 + (ii + 1) * 128],
                            t1b[tb], ident[:], start=True, stop=True),
                            reads=[("t1b", tb), "ident"], writes=[("pD", tg % 2)])
                    evac(wb[:, ee * 512:(ee + 1) * 512],
                         pD[:, (tg % 2) * 512:(tg % 2) * 512 + 512],
                         [("pD", tg % 2)], [("wbt", tg, ee)])
                yield step

        P.mark("P3d")
        CBANKS = (("pA", 0), ("pA", 1), ("pB", 0), ("pB", 1))
        if True:
            pending = {tg: list(build_wbt_steps(tg)) for tg in range(4)}
            for tg in range(4):
                wbt = wbtb[tg % 2]

                ROT4 = [("pC", 0), ("pC", 1)]

                def rw_one(ee, Isub, tg=tg, wbt=wbt):
                    par = ee % 2
                    P.emit("vector", lambda e, Isub=Isub, ee=ee, par=par,
                           wbt=wbt: e.tensor_tensor(
                        rw_bf[par][:, Isub * 512:(Isub + 1) * 512],
                        r_bf[:, Isub * 512:(Isub + 1) * 512],
                        wbt[:, ee * 512:(ee + 1) * 512],
                        ALU.mult),
                        reads=[("r", Isub), ("wbt", tg, ee)],
                        writes=[("rw", par, Isub)], cls="tt")

                def stageA(ee, tg=tg, with_rw=True):
                    par = ee % 2
                    for Isub in range(4):
                        bk = ROT4[(ee * 4 + Isub) % 2]
                        pt = PSN[bk[0]][:, bk[1] * 512:(bk[1] + 1) * 512]
                        for Lc in range(4):
                            P.emit("tensor", lambda e, ee=ee, Isub=Isub, Lc=Lc,
                                   tg=tg, pt=pt: e.matmul(
                                pt,
                                w1_sb[:, (ee * 4 + Lc) * 512 + Isub * 128:
                                      (ee * 4 + Lc) * 512 + (Isub + 1) * 128],
                                latT[:, Lc * 2048 + tg * 512:
                                     Lc * 2048 + (tg + 1) * 512],
                                start=(Lc == 0), stop=(Lc == 3)),
                                reads=["lat", ("w1", ee // 2)], writes=[bk])
                        P.emit("scalar", lambda e, Isub=Isub, pt=pt: e.activation(
                            r_bf[:, Isub * 512:(Isub + 1) * 512], pt, AF.Relu),
                            reads=[bk], writes=[("r", Isub)])
                        if with_rw:
                            rw_one(ee, Isub)

                def stageB(ee):
                    par = ee % 2
                    for Isub in range(4):
                        P.emit("vector", lambda e, Isub=Isub, par=par:
                               e.tensor_tensor(
                            r2w_bf[:, Isub * 512:(Isub + 1) * 512],
                            rw_bf[par][:, Isub * 512:(Isub + 1) * 512],
                            rw_bf[par][:, Isub * 512:(Isub + 1) * 512], ALU.mult),
                            reads=[("rw", par, Isub)], writes=[("r2w", Isub)],
                            cls="tt")
                    for Lsub in range(4):
                        for Isub in range(4):
                            bname, half = CBANKS[Lsub]
                            P.emit("tensor", lambda e, ee=ee, Isub=Isub, Lsub=Lsub,
                                   bname=bname, half=half: e.matmul(
                                PSN[bname][:, half * 512:(half + 1) * 512],
                                w2_sb[:, (ee * 4 + Isub) * 512 + Lsub * 128:
                                      (ee * 4 + Isub) * 512 + (Lsub + 1) * 128],
                                r2w_bf[:, Isub * 512:(Isub + 1) * 512],
                                start=(ee == 0 and Isub == 0),
                                stop=(ee == 7 and Isub == 3)),
                                reads=[("r2w", Isub), ("w2", ee // 2)],
                                writes=[CBANKS[Lsub]])

                nxt = pending.get(tg + 1, [])
                if tg == 0:
                    stageA(0, with_rw=False)
                    for st_ in pending[0]:
                        st_()
                    for Isub in range(4):
                        rw_one(0, Isub)
                else:
                    stageA(0)
                for ee in range(1, 8):
                    stageA(ee)
                    if nxt:
                        nxt.pop(0)()
                    stageB(ee - 1)
                stageB(7)
                if tg == 3:
                    for j in range(2):
                        P.emit("sync", lambda e, j=j: e.dma_start(
                            V_km[:, j * 4096:(j + 1) * 4096], up_bs[j, :, :]),
                            reads=WBSK("u", j),
                            writes=[("up", j), ("w1", j)], dma=True)
                        P.emit("scalar", lambda e, j=j: e.dma_start(
                            K_fm[:, 16384 + j * 4096:16384 + (j + 1) * 4096],
                            dn_bs[j, :, :]),
                            reads=WBSK("d", j),
                            writes=[("dn", j), "lat"], dma=True)
                while nxt:
                    nxt.pop(0)()
                for Lsub in range(4):
                    bname, half = CBANKS[Lsub]
                    evac(comb[:, Lsub * 2048 + tg * 512:
                              Lsub * 2048 + (tg + 1) * 512],
                         PSN[bname][:, half * 512:(half + 1) * 512],
                         [CBANKS[Lsub]], ["comb"])
        if "dbg_comb" in dbg:
            P.emit("sync", lambda e: e.dma_start(dbg["dbg_comb"][:, :], comb),
                   reads=["comb"], dma=True)

        P.mark("P3e")
        P.barrier()
        up_sb = [V_km[:, j * 4096:(j + 1) * 4096] for j in range(2)]
        dn_sb = [K_fm[:, 16384 + j * 4096:16384 + (j + 1) * 4096] for j in range(2)]
        OUTB = ("pA", "pB", "pC", "pD")

        def up_dma(sc):
            ub = sc % 2
            P.emit("sync", lambda e, sc=sc, ub=ub: e.dma_start(
                up_sb[ub][:, 0:4096], up_bs[sc, :, :]),
                reads=WBSK("u", sc), writes=[("up", ub)], dma=True)

        def dn_dma(sc):
            db = sc % 2
            P.emit("sync", lambda e, sc=sc, db=db: e.dma_start(
                dn_sb[db][:, 0:4096], dn_bs[sc, :, :]),
                reads=WBSK("d", sc), writes=[("dn", db)], dma=True)

        for tg in range(4):
            for sc in range(8):
                ub = sc % 2
                for ssub in range(4):
                    bk = (("pC", 0), ("pC", 1), ("pD", 0), ("pD", 1))[ssub]
                    pt = PSN[bk[0]][:, bk[1] * 512:(bk[1] + 1) * 512]
                    for hc in range(8):
                        P.emit("tensor", lambda e, sc=sc, ssub=ssub, hc=hc, ub=ub,
                               tg=tg, pt=pt: e.matmul(
                            pt, up_sb[ub][:, hc * 512 + ssub * 128:
                                          hc * 512 + (ssub + 1) * 128],
                            xfT[:, hc * 2048 + tg * 512: hc * 2048 + (tg + 1) * 512],
                            start=(hc == 0), stop=(hc == 7)),
                            reads=[("up", ub), ("xfT", tg * 4),
                                   ("xfT", tg * 4 + 1), ("xfT", tg * 4 + 2),
                                   ("xfT", tg * 4 + 3)], writes=[bk])
                    rb = ssub % 2
                    P.emit("scalar", lambda e, rb=rb, pt=pt: e.activation(
                        r_sh[rb], pt, AF.Relu),
                        reads=[bk], writes=[("rsh", rb)])
                    P.emit("vector", lambda e, sc=sc, ssub=ssub, rb=rb:
                           e.tensor_tensor(
                        h_sh[:, (sc * 4 + ssub) * 512:(sc * 4 + ssub + 1) * 512],
                        r_sh[rb], r_sh[rb], ALU.mult),
                        reads=[("rsh", rb)], writes=[("hsh", sc * 4 + ssub)],
                        cls="tt")
                if sc + 2 < 8:
                    up_dma(sc + 2)
                if sc < 2 and tg > 0:
                    dn_dma(sc)
            if tg + 1 < 4:
                up_dma(0)
                up_dma(1)
            for xb in range(2):
                ti = tg * 4 + 2 + xb
                P.emit("gpsimd", lambda e, ti=ti, xb=xb: e.dma_start(
                    x_t[xb][:], x1s[ti * 128:(ti + 1) * 128, :]),
                    reads=[("x1s", ti)], writes=[("x_t", xb)], dma=True)
            for sc in range(8):
                db = sc % 2
                for tsub in range(4):
                    for hf in range(2):
                        for ssub in range(4):
                            pn = OUTB[tsub]
                            first = (sc == 0 and ssub == 0)
                            P.emit("tensor", lambda e, sc=sc, ssub=ssub, tsub=tsub,
                                   hf=hf, db=db, pn=pn, first=first: e.matmul(
                                PSN[pn][:, hf * 512:(hf + 1) * 512],
                                h_sh[:, (sc * 4 + ssub) * 512 + tsub * 128:
                                     (sc * 4 + ssub) * 512 + (tsub + 1) * 128],
                                dn_sb[db][:, ssub * 1024 + hf * 512:
                                           ssub * 1024 + (hf + 1) * 512],
                                start=first, stop=False),
                                reads=[("hsh", sc * 4 + ssub), ("dn", db)],
                                writes=[(pn, hf)])
                if sc + 2 < 8:
                    dn_dma(sc + 2)
            for tsub in range(4):
                for hf in range(2):
                    for Lc in range(4):
                        pn = OUTB[tsub]
                        P.emit("tensor", lambda e, tsub=tsub, hf=hf, Lc=Lc, tg=tg,
                               pn=pn: e.matmul(
                            PSN[pn][:, hf * 512:(hf + 1) * 512],
                            comb[:, Lc * 2048 + tg * 512 + tsub * 128:
                                 Lc * 2048 + tg * 512 + (tsub + 1) * 128],
                            fc2_sb[:, Lc * 1024 + hf * 512:
                                   Lc * 1024 + (hf + 1) * 512],
                            start=False, stop=(Lc == 3)),
                            reads=["comb", "fc2"], writes=[(pn, hf)])
            for tsub in (2, 3, 0, 1):
                ti = tg * 4 + tsub
                xb = tsub % 2
                ob = tsub % 2
                pn = OUTB[tsub]
                P.emit("vector", lambda e, pn=pn, xb=xb, ob=ob: e.tensor_tensor(
                    out_sb[ob][:], PSN[pn][:], x_t[xb][:], ALU.add),
                    reads=[(pn, 0), (pn, 1), ("x_t", xb)], writes=[("out", ob)],
                    cls="tt")
                P.emit("gpsimd", lambda e, ti=ti, ob=ob: e.dma_start(
                    y[ti * 128:(ti + 1) * 128, :], out_sb[ob][:]),
                    reads=[("out", ob)], dma=True)
                if tsub >= 2:
                    tr = tg * 4 + (tsub - 2)
                    P.emit("gpsimd", lambda e, tr=tr, xb=xb: e.dma_start(
                        x_t[xb][:], x1s[tr * 128:(tr + 1) * 128, :]),
                        reads=[("x1s", tr)], writes=[("x_t", xb)], dma=True)

        P.wait_all("sync")
        global LAST_MARKS
        LAST_MARKS = list(P.phase_marks)
        block = es.enter_context(nc.Block())
        realize(P, block, sems)
    return nc



def host_inputs(inputs, stage="full", debug=()):
    x = np.asarray(inputs["L0_attn_x"], np.float32)
    t = lambda a: np.ascontiguousarray(np.asarray(a, np.float32).T)

    def tile8(wT, ncols):
        return np.ascontiguousarray(
            wT.reshape(8, 128, ncols).transpose(1, 0, 2).reshape(128, 8 * ncols))

    def tile4(wT, ncols):
        return np.ascontiguousarray(
            wT.reshape(4, 128, ncols).transpose(1, 0, 2).reshape(128, 4 * ncols))

    wqT = tile8(t(inputs["L0_attn_Wq"]), H)
    wkT = tile8(t(inputs["L0_attn_Wk"]), H)
    wvT = tile8(t(inputs["L0_attn_Wv"]), H)
    fc1T = tile8(t(inputs["L0_ffn_fc1_w"]), LAT)
    gateT = tile8(t(inputs["L0_ffn_gate"]), E)
    fc2T = tile4(t(inputs["L0_ffn_fc2_w"]), H)
    upT = np.ascontiguousarray(
        t(inputs["L0_ffn_up_w"]).reshape(8, 128, 8, 512)
        .transpose(2, 1, 0, 3).reshape(8, 128, 4096))
    downT = np.ascontiguousarray(
        t(inputs["L0_ffn_down_w"]).reshape(8, 4, 128, 1024)
        .transpose(0, 2, 1, 3).reshape(8, 128, 4096))
    w1T = np.ascontiguousarray(np.transpose(
        np.asarray(inputs["L0_ffn_w1"], np.float32), (0, 2, 1))
        .reshape(8, 4, 128, 512).transpose(2, 0, 1, 3).reshape(128, 16384))
    w2T = np.ascontiguousarray(np.transpose(
        np.asarray(inputs["L0_ffn_w2"], np.float32), (0, 2, 1))
        .reshape(8, 4, 128, 512).transpose(2, 0, 1, 3).reshape(128, 16384))
    cb = np.asarray(inputs["L0_ffn_corr_bias"], np.float32)
    ident = np.eye(128, dtype=ml_dtypes.bfloat16)
    cb128 = np.ascontiguousarray(
        np.broadcast_to(np.repeat(cb, NT)[None, :], (128, 128))).astype(np.float32)
    pp = np.arange(128)[:, None]
    ccol = np.arange(512)[None, :]
    mask4 = [(ccol <= 128 * j + pp).astype(ml_dtypes.bfloat16) for j in range(4)]
    in_maps = []
    for c in range(8):
        b, d = c // 2, c % 2
        xs = np.ascontiguousarray(x[b])
        xq = np.ascontiguousarray(x[b].reshape(NKT, 128, H)[d::2].reshape(TCORE, H))
        masks = np.stack([mask4[d], mask4[2 + d]])
        qtg = np.arange(NT, dtype=np.int32)[None, :] * 2 + d
        in_maps.append({
            "xseq": xs, "xq": xq, "qtile_g": qtg,
            "wqT": wqT, "wkT": wkT, "wvT": wvT,
            "fc1T": fc1T, "gateT": gateT, "fc2T": fc2T, "upT": upT,
            "downT": downT, "w1T": w1T, "w2T": w2T, "ident": ident,
            "masks": np.ascontiguousarray(masks), "cb128": cb128,
        })
    return in_maps


def kernel(**inputs):
    nc = build_program(stage="full")
    in_maps = host_inputs(inputs)
    res = bass_utils.run_bass_kernel_spmd(nc, in_maps, core_ids=list(range(8)))
    out = np.empty((B, S, H), np.float32)
    for c in range(8):
        b, d = c // 2, c % 2
        yc = np.asarray(res.results[c]["y"], np.float32)
        out[b].reshape(NKT, 128, H)[d::2] = yc.reshape(NT, 128, H)
    return out


# version 44
# speedup vs baseline: 1.0290x; 1.0290x over previous
import numpy as np
from contextlib import ExitStack

import concourse.bass as bass
import concourse.mybir as mybir
import concourse.bass_utils as bass_utils
import ml_dtypes

F32 = mybir.dt.float32
BF16 = mybir.dt.bfloat16
AF = mybir.ActivationFunctionType
ALU = mybir.AluOpType
AX = mybir.AxisListType.X

H = 1024
S = 4096
B = 4
NT = 16
TCORE = NT * 128
NKT = 32
LAT = 512
E = 8

ENGS = ("tensor", "vector", "scalar", "sync", "gpsimd")
DMA_Q = {"sync": "Dsy", "gpsimd": "Dgp", "scalar": "Dsc"}
QUEUES = ("Dsy", "Dgp", "Dsc")
UNSAFE = ("tt", "reduce", "recip", "accum")


NSLOT = 12


class Plan:

    def __init__(self):
        self.ops = {e: [] for e in ENGS}
        self.ncomp = {e: 0 for e in ENGS}
        self.ndma = {q: 0 for q in QUEUES}
        self.uses = {q: [0] * NSLOT for q in QUEUES}
        self.refd = {e: set() for e in ENGS}
        self.last_write = {}
        self.readers = {}
        self.prev_op = {e: None for e in ENGS}
        self.phase = ""
        self.phase_marks = []

    def mark(self, name):
        self.phase = name
        self.phase_marks.append((name, self.ncomp["tensor"]))

    def emit(self, eng, fn, reads=(), writes=(), cls="safe", dma=False, tag=""):
        q = DMA_Q[eng] if dma else eng
        waits = set()

        deferred = eng in ("vector", "scalar", "gpsimd")

        def need_wait(tok):
            return tok[0] == "Q" or tok[0] != eng or deferred

        for r in reads:
            lw = self.last_write.get(r)
            if lw is not None and need_wait(lw[0]):
                waits.add(lw[0])
        for w in writes:
            lw = self.last_write.get(w)
            if lw is not None and need_wait(lw[0]):
                waits.add(lw[0])
            for rd in self.readers.get(w, ()):
                if need_wait(rd):
                    waits.add(rd)
        if dma:
            slot = self.ndma[q] % NSLOT
            self.ndma[q] += 1
            if self.uses[q][slot] > 0:
                waits.add(("Q", q, slot, self.uses[q][slot]))
            self.uses[q][slot] += 1
            tok = ("Q", q, slot, self.uses[q][slot])
            self.ops[eng].append((fn, waits, (q, slot), tag))
        else:
            idx = self.ncomp[eng]
            self.ncomp[eng] += 1
            tok = (q, idx)
            self.ops[eng].append((fn, waits, False, tag))
        for w in writes:
            self.last_write[w] = (tok, cls)
            self.readers[w] = []
        for r in reads:
            self.readers.setdefault(r, []).append(tok)
        return tok

    def wait_all(self, eng):
        toks = set()
        for e in ENGS:
            if e != eng and self.ncomp[e]:
                toks.add((e, self.ncomp[e] - 1))
        for q in QUEUES:
            for s in range(NSLOT):
                if self.uses[q][s]:
                    toks.add(("Q", q, s, self.uses[q][s]))
        self.ops[eng].append((None, toks, False, "wait_all"))

    def barrier(self):
        for e in ENGS:
            self.wait_all(e)


def realize(plan, block, sems):
    for e in ENGS:
        for (fn, waits, dma, tag) in plan.ops[e]:
            for t in waits:
                if t[0] != "Q":
                    plan.refd[t[0]].add(t[1])
    incmap = {}
    for e in ENGS:
        vals = {}
        c = 0
        for idx in sorted(plan.refd[e]):
            c += 1
            vals[idx] = c
        incmap[e] = vals

    def run_engine(eng_obj, eng_name):
        cidx = 0
        waited = {}
        for (fn, waits, dma, tag) in plan.ops[eng_name]:
            best = {}
            for t in waits:
                if t[0] == "Q":
                    key = ("Q", t[1], t[2])
                    v = 16 * t[3]
                else:
                    key = t[0]
                    v = incmap[t[0]].get(t[1])
                    assert v is not None, (eng_name, t, tag)
                if best.get(key, -1) < v:
                    best[key] = v
            for key in sorted(best, key=repr):
                v = best[key]
                if waited.get(key, -1) >= v:
                    continue
                eng_obj.wait_ge(sems[key], v)
                waited[key] = v
            if fn is None:
                continue
            instr = fn(eng_obj)
            if dma:
                q, slot = dma
                instr.then_inc(sems[("Q", q, slot)], 16)
            else:
                if cidx in incmap[eng_name]:
                    instr.then_inc(sems[eng_name], 1)
                cidx += 1

    block.tensor(lambda t: run_engine(t, "tensor"))
    block.vector(lambda v: run_engine(v, "vector"))
    block.scalar(lambda s: run_engine(s, "scalar"))
    block.sync(lambda sy: run_engine(sy, "sync"))
    block.gpsimd(lambda g: run_engine(g, "gpsimd"))


def build_program(stage="full", debug=()):
    nc = bass.Bass("TRN2", target_bir_lowering=False)

    xseq = nc.dram_tensor("xseq", [S, H], F32, kind="ExternalInput").ap()
    xq = nc.dram_tensor("xq", [TCORE, H], F32, kind="ExternalInput").ap()
    wqT = nc.dram_tensor("wqT", [128, 8 * H], F32, kind="ExternalInput").ap()
    wkT = nc.dram_tensor("wkT", [128, 8 * H], F32, kind="ExternalInput").ap()
    wvT = nc.dram_tensor("wvT", [128, 8 * H], F32, kind="ExternalInput").ap()
    fc1Td = nc.dram_tensor("fc1T", [128, 8 * LAT], F32, kind="ExternalInput").ap()
    gateTd = nc.dram_tensor("gateT", [128, 8 * E], F32, kind="ExternalInput").ap()
    fc2Td = nc.dram_tensor("fc2T", [128, 4 * H], F32, kind="ExternalInput").ap()
    upTd = nc.dram_tensor("upT", [8, 128, 4096], F32, kind="ExternalInput").ap()
    downTd = nc.dram_tensor("downT", [8, 128, 4096], F32, kind="ExternalInput").ap()
    w1Td = nc.dram_tensor("w1T", [128, 16384], F32, kind="ExternalInput").ap()
    w2Td = nc.dram_tensor("w2T", [128, 16384], F32, kind="ExternalInput").ap()
    ident_in = nc.dram_tensor("ident", [128, 128], BF16, kind="ExternalInput").ap()
    masks_in = nc.dram_tensor("masks", [2, 128, 512], BF16, kind="ExternalInput").ap()
    cb_in = nc.dram_tensor("cb128", [128, 128], F32, kind="ExternalInput").ap()

    y = nc.dram_tensor("y", [TCORE, H], F32, kind="ExternalOutput").ap()
    x1s = nc.dram_tensor("x1s", [TCORE, H], F32, kind="Internal").ap()
    xfts = nc.dram_tensor("xfts", [NKT, 128, H], BF16, kind="Internal").ap()
    qtg_in = nc.dram_tensor("qtile_g", [1, NT], mybir.dt.int32,
                            kind="ExternalInput").ap()
    up_bs = nc.dram_tensor("up_bs", [8, 128, 4096], BF16, kind="Internal").ap()
    wq_bs = nc.dram_tensor("wq_bs", [128, 8192], BF16, kind="Internal").ap()
    dn_bs = nc.dram_tensor("dn_bs", [8, 128, 4096], BF16, kind="Internal").ap()
    qs = nc.dram_tensor("qs", [NT, 128, H], BF16, kind="Internal").ap()
    dbg = {}
    for name, shape, dt in (("dbg_k", [128, 8 * S], BF16),
                            ("dbg_v", [128, NKT * H], BF16),
                            ("dbg_q", [NT, 128, H], BF16),
                            ("dbg_lat", [128, 4 * TCORE], BF16),
                            ("dbg_wr", [128, 128], F32),
                            ("dbg_comb", [128, 4 * TCORE], BF16)):
        if name in debug:
            dbg[name] = nc.dram_tensor(name, shape, dt, kind="ExternalOutput").ap()

    P = Plan()

    with ExitStack() as es:
        sb = lambda name, shape, dt: es.enter_context(nc.sbuf_tensor(name, shape, dt))
        ident = sb("identt", [128, 128], BF16)
        masks = sb("maskst", [128, 2 * 512], BF16)
        stt_names = ("sx", "ssq", "mean", "vq", "m2", "var", "r0", "r2", "hv",
                     "wn", "rstd", "negmr")
        stA = {k: sb(f"st_{k}", [128, NT], F32) for k in stt_names}
        x_t = [sb(f"x_t{j}", [128, H], F32) for j in range(2)]
        xf_t = [sb(f"xf_t{j}", [128, H], BF16) for j in range(2)]
        xfTg = [sb(f"xfTg{j}", [128, 8 * 512], BF16) for j in range(2)]
        wkv_sb = sb("wkv_sb", [128, 16 * 1024], BF16)
        qtg_sb = sb("qtg_sb", [1, NT], mybir.dt.int32)
        x1_t = [sb(f"x1_t{j}", [128, H], F32) for j in range(2)]
        q_sb = [wkv_sb[:, j * 1024:(j + 1) * 1024] for j in range(2)]
        P_t = wkv_sb[:, 2048:2048 + 2048]
        PT_t = wkv_sb[:, 4096:4096 + 2048]

        psm = lambda name: es.enter_context(nc.psum_tensor(name, [128, 1024], F32))
        pA, pB, pC, pD = psm("pA"), psm("pB"), psm("pC"), psm("pD")
        PSN = {"pA": pA, "pB": pB, "pC": pC, "pD": pD}

        sems = {}
        for k in ENGS:
            sems[k] = es.enter_context(nc.semaphore(f"sem_{k}"))
        for q in QUEUES:
            for s in range(NSLOT):
                sems[("Q", q, s)] = es.enter_context(
                    nc.semaphore(f"sem_{q}_{s}"))

        evac_rr = [0]

        def evac(dst, srcp, reads, writes):
            evac_rr[0] ^= 1
            if evac_rr[0]:
                P.emit("vector", lambda e, dst=dst, srcp=srcp: e.tensor_copy(
                    dst, srcp), reads=reads, writes=writes)
            else:
                P.emit("scalar", lambda e, dst=dst, srcp=srcp: e.copy(
                    dst, srcp), reads=reads, writes=writes)

        P.emit("sync", lambda e: e.dma_start(ident[:], ident_in[:, :]),
               writes=["ident"], dma=True)
        P.emit("gpsimd", lambda e: e.dma_start(qtg_sb[:], qtg_in[:, :]),
               writes=["qtg"], dma=True)


        def load_wT(dst_sb, src_ap, ncols, key, nchunks, reads=()):
            P.emit("gpsimd", lambda e: e.dma_start(
                dst_sb[:, 0:nchunks * ncols], src_ap[:, :]),
                reads=reads, writes=[key], dma=True)



        def rstd_chain(c0, c1):
            c = lambda k: stA[k][:, c0:c1]
            rk = lambda *ks: [("st", k, j) for k in ks for j in range(c0, c1)]
            P.emit("vector", lambda e, c=c: e.tensor_scalar(
                c("mean"), c("sx"), 1.0 / H, None, ALU.mult),
                reads=rk("sx"), writes=rk("mean"))
            P.emit("vector", lambda e, c=c: e.tensor_scalar(
                c("vq"), c("ssq"), 1.0 / H, 1e-5, ALU.mult, ALU.add),
                reads=rk("ssq"), writes=rk("vq"))
            P.emit("vector", lambda e, c=c: e.scalar_tensor_tensor(
                c("m2"), c("mean"), 1.0, c("mean"), ALU.mult, ALU.mult),
                reads=rk("mean"), writes=rk("m2"))
            P.emit("vector", lambda e, c=c: e.tensor_tensor(
                c("var"), c("vq"), c("m2"), ALU.subtract),
                reads=rk("vq", "m2"), writes=rk("var"), cls="tt")
            P.emit("scalar", lambda e, c=c: e.activation(c("wn"), c("var"), AF.Sqrt),
                   reads=rk("var"), writes=rk("wn"))
            P.emit("vector", lambda e, c=c: e.reciprocal(c("r0"), c("wn")),
                   reads=rk("wn"), writes=rk("r0"), cls="recip")
            P.emit("vector", lambda e, c=c: e.scalar_tensor_tensor(
                c("r2"), c("r0"), 1.0, c("r0"), ALU.mult, ALU.mult),
                reads=rk("r0"), writes=rk("r2"))
            P.emit("vector", lambda e, c=c: e.scalar_tensor_tensor(
                c("hv"), c("r2"), -0.5, c("var"), ALU.mult, ALU.mult),
                reads=rk("r2", "var"), writes=rk("hv"))
            P.emit("vector", lambda e, c=c: e.tensor_scalar(
                c("wn"), c("hv"), 1.5, None, ALU.add),
                reads=rk("hv"), writes=rk("wn"))
            P.emit("vector", lambda e, c=c: e.scalar_tensor_tensor(
                c("rstd"), c("wn"), 1.0, c("r0"), ALU.mult, ALU.mult),
                reads=rk("wn", "r0"), writes=rk("rstd"))
            P.emit("vector", lambda e, c=c: e.scalar_tensor_tensor(
                c("negmr"), c("mean"), -1.0, c("rstd"), ALU.mult, ALU.mult),
                reads=rk("mean", "rstd"), writes=rk("negmr"))

        def normalize(xbuf, xfbuf, col, xkey, xfkey):
            P.emit("vector", lambda e: e.tensor_scalar(
                xf_t[xfbuf][:], x_t[xbuf][:], stA["rstd"][:, col:col + 1],
                stA["negmr"][:, col:col + 1], ALU.mult, ALU.add),
                reads=[xkey, ("st", "rstd", col), ("st", "negmr", col)],
                writes=[xfkey])


        def transpose_tile(xfbuf, xfkey, gslot, j, pt, pname):
            gk = ("xfTg", gslot)
            for half in range(2):
                bk = (pname, half)
                for cc in range(4):
                    hc = half * 4 + cc
                    P.emit("tensor", lambda e, hc=hc, half=half, cc=cc: e.matmul(
                        pt[:, half * 512 + cc * 128: half * 512 + (cc + 1) * 128],
                        xf_t[xfbuf][:, hc * 128:(hc + 1) * 128], ident[:],
                        start=True, stop=True),
                        reads=[xfkey, "ident"], writes=[bk])
            for half in range(2):
                bk = (pname, half)
                dst = xfTg[gslot][:].rearrange("p (hc f) -> p hc f", hc=8)[
                    :, half * 4:(half + 1) * 4, j * 128:(j + 1) * 128]
                srcp = pt[:, half * 512:(half + 1) * 512].rearrange(
                    "p (c f) -> p c f", c=4)
                evac(dst, srcp, [bk], [gk])

        wq_sb = wkv_sb[:, 0:8192]
        wk_sb = wkv_sb[:, 0:8192]
        wv_sb = wkv_sb[:, 8192:16384]
        K_fm = sb("K_fm", [128, 8 * S], BF16)
        V_km = sb("V_km", [128, NKT * H], BF16)

        def ln_stats2(xbuf, col, xkey):
            P.emit("vector", lambda e, xbuf=xbuf, col=col: e.scalar_tensor_tensor(
                masks[:], xpool[xbuf][:], 1.0, xpool[xbuf][:], ALU.mult, ALU.mult,
                accum_out=stA["ssq"][:, col:col + 1]),
                reads=[xkey], writes=["masks", ("st", "ssq", col)], cls="accum")
            P.emit("scalar", lambda e, xbuf=xbuf, col=col: e.activation(
                xpool[xbuf][:], xpool[xbuf][:], AF.Copy,
                accum_out=stA["sx"][:, col:col + 1]),
                reads=[xkey], writes=[xkey, ("st", "sx", col)], cls="accum")

        def normalize2(xbuf, xfbuf, col, xkey, xfkey):
            P.emit("vector", lambda e, xbuf=xbuf, xfbuf=xfbuf, col=col:
                   e.tensor_scalar(
                xf_t[xfbuf][:], xpool[xbuf][:], stA["rstd"][:, col:col + 1],
                stA["negmr"][:, col:col + 1], ALU.mult, ALU.add),
                reads=[xkey, ("st", "rstd", col), ("st", "negmr", col)],
                writes=[xfkey])

        xpool = x_t + x1_t

        def sweep(G, src_fn, proj_fn):
            def S1(g):
                for j in range(4):
                    t = g * 4 + j
                    xb = t % 4
                    ap, _ = src_fn(t)
                    if g > 0:
                        P.emit("sync", lambda e, ap=ap, xb=xb: e.dma_start(
                            xpool[xb][:], ap), writes=[("xp", xb)], dma=True)
                    ln_stats2(xb, (g % 4) * 4 + j, ("xp", xb))

            def S2(g):
                rstd_chain((g % 4) * 4, (g % 4) * 4 + 4)

            def S3(g):
                gslot = g % 2
                for j in range(4):
                    xb = (g * 4 + j) % 4
                    normalize2(xb, j % 2, (g % 4) * 4 + j, ("xp", xb),
                               ("xf_t", j % 2))
                    transpose_tile(j % 2, ("xf_t", j % 2), gslot, j, pC, "pC")

            S1(0)
            S2(0)
            for g in range(G):
                S3(g)
                if g + 1 < G:
                    S1(g + 1)
                    S2(g + 1)
                proj_fn(g, g % 2)

        ROT6 = [("pA", 0), ("pA", 1), ("pB", 0), ("pB", 1), ("pD", 0), ("pD", 1)]
        rotc = [0]

        def rot_bank():
            bk = ROT6[rotc[0] % 6]
            rotc[0] += 1
            return bk, PSN[bk[0]][:, bk[1] * 512:(bk[1] + 1) * 512]

        for t0_ in range(4):
            P.emit("sync", lambda e, t0_=t0_: e.dma_start(
                xpool[t0_][:], xseq[t0_ * 128:(t0_ + 1) * 128, :]),
                writes=[("xp", t0_)], dma=True)
        P.last_write["p1go"] = P.last_write[("xp", 3)]
        load_wT(wk_sb, wkT, H, "wk", 8, reads=["p1go"])
        load_wT(wv_sb, wvT, H, "wv", 8)
        def wq_precast():
            for hh in range(8):
                P.emit("gpsimd", lambda e, hh=hh: e.dma_start(
                    wq_bs[:, hh * 1024:(hh + 1) * 1024],
                    wqT[:, hh * 1024:(hh + 1) * 1024]),
                    writes=[("wqbs", hh)], dma=True)

        def proj_kv(kg, gslot):
            if kg == 2:
                wq_precast()
            gk = ("xfTg", gslot)
            for j in range(4):
                kt = kg * 4 + j
                srcp = xfTg[gslot][:].rearrange("p (hc f) -> p hc f", hc=8)[
                    :, :, j * 128:(j + 1) * 128]
                P.emit("scalar", lambda e, kt=kt, srcp=srcp: e.dma_start(
                    xfts[kt, :, :], srcp),
                    reads=[gk], writes=[("xfts", kt)], dma=True)
            if kg == 7:
                do_V(kg, gslot)
                do_K(kg, gslot)
            else:
                do_K(kg, gslot)
                do_V(kg, gslot)

        def do_K(kg, gslot):
            gk = ("xfTg", gslot)
            for fc in range(8):
                bk, pt = rot_bank()
                for hc in range(8):
                    P.emit("tensor", lambda e, fc=fc, hc=hc, pt=pt,
                           gslot=gslot: e.matmul(
                        pt, wk_sb[:, hc * H + fc * 128: hc * H + (fc + 1) * 128],
                        xfTg[gslot][:, hc * 512:(hc + 1) * 512],
                        start=(hc == 0), stop=(hc == 7)),
                        reads=[gk, "wk"], writes=[bk])
                evac(K_fm[:, fc * S + kg * 512: fc * S + (kg + 1) * 512], pt,
                     [bk], [("K", kg)])

        def do_V(kg, gslot):
            gk = ("xfTg", gslot)
            for j in range(4):
                kc = kg * 4 + j
                for hf in range(2):
                    bk, pt = rot_bank()
                    for hc in range(8):
                        P.emit("tensor", lambda e, j=j, hf=hf, hc=hc, pt=pt,
                               gslot=gslot: e.matmul(
                            pt, xfTg[gslot][:, hc * 512 + j * 128:
                                            hc * 512 + (j + 1) * 128],
                            wv_sb[:, hc * H + hf * 512: hc * H + (hf + 1) * 512],
                            start=(hc == 0), stop=(hc == 7)),
                            reads=[gk, "wv"], writes=[bk])
                    evac(V_km[:, kc * H + hf * 512: kc * H + (hf + 1) * 512],
                         pt, [bk], [("V", kc)])

        sweep(8, lambda t: (xseq[t * 128:(t + 1) * 128, :], None), proj_kv)
        if "dbg_k" in dbg:
            P.emit("sync", lambda e: e.dma_start(dbg["dbg_k"][:, :], K_fm[:]),
                   reads=[("K", kg2) for kg2 in range(8)], dma=True)
        if "dbg_v" in dbg:
            P.emit("sync", lambda e: e.dma_start(dbg["dbg_v"][:, :], V_km[:]),
                   reads=[("V", kc2) for kc2 in range(32)], dma=True)

        wq_sb2 = wkv_sb[:, 8192:16384]
        q_ev = wkv_sb[:, 2048:6144]

        QREGS = {}

        def qload(i, gslot):
            j = i % 4
            qb = i % 2

            def fn(e, i=i, qb=qb):
                if "r" not in QREGS:
                    QREGS["r"] = (e.alloc_register("qg_r"),
                                  e.alloc_register("qg_o"))
                gr, go = QREGS["r"]
                e.reg_load(gr, qtg_sb[:1, i:i + 1])
                e.reg_mul(go, gr, 128 * 1024)
                srcp = bass.AP(xfts.tensor, go, [[1024, 128], [1, 1024]])
                return e.dma_start(xf_t[qb][:], srcp)

            P.emit("gpsimd", fn,
                   reads=["qtg", ("xfts", 2 * i), ("xfts", 2 * i + 1)],
                   writes=[("xf_t", qb)], dma=True)
            dst = xfTg[gslot][:].rearrange("p (hc f) -> p hc f", hc=8)[
                :, :, j * 128:(j + 1) * 128]
            srcp2 = xf_t[qb][:].rearrange("p (hc f) -> p hc f", hc=8)
            P.emit("vector", lambda e, dst=dst, srcp2=srcp2: e.tensor_copy(
                dst, srcp2),
                reads=[("xf_t", qb)], writes=[("xfTg", gslot)])

        for j in range(4):
            qload(j, 0)
        P.emit("sync", lambda e: e.dma_start(
            wq_sb2[:, 0:8192], wq_bs[:, :]),
            reads=[("wqbs", hh) for hh in range(8)],
            writes=["wq", "wv"], dma=True)
        P.mark("P1q")

        def proj_q(qg, gslot):
            gk = ("xfTg", gslot)
            for fc in range(8):
                bk, pt = rot_bank()
                for hc in range(8):
                    P.emit("tensor", lambda e, fc=fc, hc=hc, pt=pt,
                           gslot=gslot: e.matmul(
                        pt, wq_sb2[:, hc * H + fc * 128: hc * H + (fc + 1) * 128],
                        xfTg[gslot][:, hc * 512:(hc + 1) * 512],
                        start=(hc == 0), stop=(hc == 7)),
                        reads=[gk, "wq"], writes=[bk])
                dst = q_ev[:].rearrange("p (j f) -> p j f", j=4)[
                    :, :, fc * 128:(fc + 1) * 128]
                srcp = pt.rearrange("p (c q) -> p c q", c=4)
                P.emit("vector", lambda e, dst=dst, srcp=srcp: e.tensor_scalar(
                    dst, srcp, 1.0 / 32.0, None, ALU.mult),
                    reads=[bk], writes=["q_ev", "wk"])
            for j in range(4):
                i = qg * 4 + j
                P.emit("sync", lambda e, i=i, j=j: e.dma_start(
                    qs[i, :, :], q_ev[:, j * H:(j + 1) * H]),
                    reads=["q_ev"], writes=[("qs", i)], dma=True)

        for qg in range(4):
            if qg + 1 < 4:
                for j in range(4):
                    qload((qg + 1) * 4 + j, (qg + 1) % 2)
            proj_q(qg, qg % 2)
            if qg == 2:
                for i_ in (0, 1):
                    p_ = i_ % 2
                    for fc in range(8):
                        P.emit("tensor", lambda e, p_=p_, fc=fc: e.matmul(
                            pC[:, p_ * 512:(p_ + 1) * 512],
                            q_sb[p_][:, fc * 128:(fc + 1) * 128],
                            K_fm[:, fc * S: fc * S + 512],
                            start=(fc == 0), stop=(fc == 7)),
                            reads=[("q_sb", p_), ("K", 0)],
                            writes=[("pC", p_)])
            if qg == 0:
                for j in range(2):
                    P.emit("gpsimd", lambda e, j=j: e.dma_start(
                        masks[:, j * 512:(j + 1) * 512], masks_in[j, :, :]),
                        writes=[("masks", j), "masks"], dma=True)
                for i in (0, 1):
                    p = i % 2
                    P.emit("sync", lambda e, i=i, p=p: e.dma_start(
                        q_sb[p][:], qs[i, :, :]),
                        reads=[("qs", i)], writes=[("q_sb", p), "wk"], dma=True)
                    P.emit("scalar", lambda e, i=i, p=p: e.dma_start(
                        x_t[p][:], xq[i * 128:(i + 1) * 128, :]),
                        writes=[("x_t", p), ("xp", p)], dma=True)
        if "dbg_q" in dbg:
            for i in range(NT):
                P.emit("sync", lambda e, i=i: e.dma_start(
                    dbg["dbg_q"][i, :, :], qs[i, :, :]),
                    reads=[("qs", i)], dma=True)

        if stage == "p1":
            P.wait_all("sync")
            block = es.enter_context(nc.Block())
            realize(P, block, sems)
            return nc

        P.barrier()
        P.mark("P2")
        WBSK = lambda kind, sc: [("wbs", kind, sc, hh) for hh in range(4)]

        def precast_ud():
            for kind, srcd, dstd in (("u", upTd, up_bs), ("d", downTd, dn_bs)):
                for sc in range(8):
                    for hh in range(4):
                        P.emit("gpsimd", lambda e, sc=sc, hh=hh, srcd=srcd,
                               dstd=dstd: e.dma_start(
                            dstd[sc, :, hh * 1024:(hh + 1) * 1024],
                            srcd[sc, :, hh * 1024:(hh + 1) * 1024]),
                            writes=[("wbs", kind, sc, hh)], dma=True)

        nmkb = sb("nmkb", [128, 4], F32)
        nmv = [[sb(f"nm{p}{j}", [128, 1], F32) for j in range(2)]
               for p in range(2)]
        msc = sb("msc", [128, 8], F32)
        l_run = [sb(f"l_run{p}", [128, 1], F32) for p in range(2)]
        lparts = sb("lparts", [128, 4], F32)
        ls_t = [sb(f"ls_t{p}", [128, 1], F32) for p in range(2)]
        dtl = [sb(f"dtl{p}", [128, 1], F32) for p in range(2)]
        alpha = [sb(f"alpha{p}", [128, 1], F32) for p in range(2)]
        rl_t = [sb(f"rl_t{p}", [128, 1], F32) for p in range(2)]

        SBANKP = ((("pA", 0), ("pA", 1)), (("pB", 0), ("pB", 1)))
        SPS = (pA, pB)
        OBANK = (pD, pC)
        OKEY = ((("pD", 0), ("pD", 1)), (("pC", 0), ("pC", 1)))

        def sc_bank(i, kb):
            if i < 2:
                return ("pC", i % 2)
            return SBANKP[i % 2][kb]

        def p2_scores(i, sbi):
            p = i % 2
            L = i // 2 + 1
            kbs = min(2, L - 2 * sbi)
            for kb in range(kbs):
                ka = 2 * sbi + kb
                bk = sc_bank(i, kb)
                pt = PSN[bk[0]][:, bk[1] * 512:(bk[1] + 1) * 512]
                for fc in range(8):
                    P.emit("tensor", lambda e, fc=fc, ka=ka, pt=pt, p=p: e.matmul(
                        pt, q_sb[p][:, fc * 128:(fc + 1) * 128],
                        K_fm[:, fc * S + ka * 512: fc * S + (ka + 1) * 512],
                        start=(fc == 0), stop=(fc == 7)),
                        reads=[("q_sb", p), ("K", ka)], writes=[bk])

        def p2_soft(i, sbi):
            p = i % 2
            L = i // 2 + 1
            kbs = min(2, L - 2 * sbi)
            rb0 = sc_bank(i, 0)
            rsrc = PSN[rb0[0]][:, rb0[1] * 512: rb0[1] * 512 + kbs * 512]
            P.emit("vector", lambda e, rsrc=rsrc: e.reduce_max(
                nmkb[:, p * 2:p * 2 + 1], rsrc, axis=AX,
                negate=True),
                reads=[sc_bank(i, 0), sc_bank(i, 1)], writes=[("nmkb", p, 0)],
                cls="reduce")
            items = [(nmkb[:, p * 2:p * 2 + 1], ("nmkb", p, 0))]
            if sbi > 0:
                items.append((nmv[p][(sbi - 1) % 2][:], ("nm", p, (sbi - 1) % 2)))
            nm_new = nmv[p][sbi % 2]
            nmk = ("nm", p, sbi % 2)
            if len(items) == 1:
                P.emit("vector", lambda e, a=items[0], nm_new=nm_new:
                       e.tensor_copy(nm_new[:], a[0]),
                       reads=[items[0][1]], writes=[nmk])
            elif len(items) == 2:
                P.emit("vector", lambda e, a=items[0], b_=items[1],
                       nm_new=nm_new: e.tensor_tensor(
                    nm_new[:], a[0], b_[0], ALU.min),
                    reads=[items[0][1], items[1][1]], writes=[nmk], cls="tt")
            else:
                P.emit("vector", lambda e, a=items[0], b_=items[1]: e.tensor_tensor(
                    msc[:, p:p + 1], a[0], b_[0], ALU.min),
                    reads=[items[0][1], items[1][1]], writes=[("msc", p)],
                    cls="tt")
                P.emit("vector", lambda e, c_=items[2], nm_new=nm_new, p=p:
                       e.tensor_tensor(
                    nm_new[:], msc[:, p:p + 1], c_[0], ALU.min),
                    reads=[("msc", p), items[2][1]], writes=[nmk], cls="tt")
            if sbi > 0:
                nm_old = nmv[p][(sbi - 1) % 2]
                P.emit("vector", lambda e, nm_new=nm_new, nm_old=nm_old, p=p:
                       e.tensor_tensor(dtl[p][:], nm_new[:], nm_old[:],
                                       ALU.subtract),
                       reads=[nmk, ("nm", p, (sbi - 1) % 2)], writes=[("dtl", p)],
                       cls="tt")
                P.emit("scalar", lambda e, p=p: e.activation(
                    alpha[p][:], dtl[p][:], AF.Exp),
                    reads=[("dtl", p)], writes=[("alpha", p)])
                P.emit("vector", lambda e, p=p: e.tensor_scalar_mul(
                    l_run[p][:], l_run[p][:], alpha[p][:]),
                    reads=[("l_run", p), ("alpha", p)], writes=[("l_run", p)])
                P.emit("scalar", lambda e, p=p: e.mul(
                    OBANK[p][:], OBANK[p][:], alpha[p][:]),
                    reads=[OKEY[p][0], OKEY[p][1], ("alpha", p)],
                    writes=[OKEY[p][0], OKEY[p][1]])
            for kb in range(kbs):
                ka = 2 * sbi + kb
                diag = (ka == L - 1)
                bk = sc_bank(i, kb)
                pt = PSN[bk[0]][:, bk[1] * 512:(bk[1] + 1) * 512]
                acc = None if diag else lparts[:, p * 2 + kb:p * 2 + kb + 1]
                pcol = p * 1024 + kb * 512
                P.emit("scalar", lambda e, pt=pt, acc=acc, nm_new=nm_new,
                       pcol=pcol: e.activation(
                    P_t[:, pcol:pcol + 512], pt, AF.Exp,
                    bias=nm_new[:], scale=1.0, accum_out=acc),
                    reads=[bk, nmk],
                    writes=[("P", p, kb)] + ([] if diag else [("lp", p, kb)]),
                    cls="accum")
                if diag:
                    P.emit("vector", lambda e, pcol=pcol, p=p: e.tensor_tensor(
                        P_t[:, pcol:pcol + 512], P_t[:, pcol:pcol + 512],
                        masks[:, p * 512:(p + 1) * 512], ALU.mult),
                        reads=[("P", p, kb), ("masks", p)], writes=[("P", p, kb)],
                        cls="tt")
                    P.emit("vector", lambda e, pcol=pcol, p=p, kb=kb: e.reduce_sum(
                        lparts[:, p * 2 + kb:p * 2 + kb + 1],
                        P_t[:, pcol:pcol + 512], axis=AX),
                        reads=[("P", p, kb)], writes=[("lp", p, kb)], cls="reduce")
            if kbs > 1:
                P.emit("vector", lambda e, p=p: e.reduce_sum(
                    ls_t[p][:], lparts[:, p * 2:p * 2 + 2], axis=AX),
                    reads=[("lp", p, 0), ("lp", p, 1)], writes=[("ls", p)],
                    cls="reduce")
                ls_ap, ls_key = ls_t[p][:], ("ls", p)
            else:
                ls_ap, ls_key = lparts[:, p * 2:p * 2 + 1], ("lp", p, 0)
            if sbi == 0:
                P.emit("vector", lambda e, ls_ap=ls_ap, p=p: e.tensor_copy(
                    l_run[p][:], ls_ap), reads=[ls_key], writes=[("l_run", p)])
            else:
                P.emit("vector", lambda e, ls_ap=ls_ap, p=p: e.tensor_tensor(
                    l_run[p][:], l_run[p][:], ls_ap, ALU.add),
                    reads=[("l_run", p), ls_key], writes=[("l_run", p)], cls="tt")

        def p2_ptav(i, sbi):
            p = i % 2
            L = i // 2 + 1
            kbs = min(2, L - 2 * sbi)
            for kb in range(kbs):
                ka = 2 * sbi + kb
                bk = sc_bank(i, kb)
                pcol = p * 1024 + kb * 512
                for cc in range(4):
                    P.emit("tensor", lambda e, cc=cc, bk=bk, pcol=pcol, p=p:
                           e.matmul(
                        PSN[bk[0]][:, bk[1] * 512 + cc * 128:
                                   bk[1] * 512 + (cc + 1) * 128],
                        P_t[:, pcol + cc * 128: pcol + (cc + 1) * 128],
                        ident[:], start=True, stop=True),
                        reads=[("P", p, kb), "ident"], writes=[bk])
                evac(PT_t[:, pcol:pcol + 512],
                     PSN[bk[0]][:, bk[1] * 512:(bk[1] + 1) * 512],
                     [bk], [("PT", p, kb)])
                for hf in range(2):
                    for cc in range(4):
                        kc = ka * 4 + cc
                        first = (sbi == 0 and kb == 0 and cc == 0)
                        last = (kb == kbs - 1 and cc == 3)
                        P.emit("tensor", lambda e, cc=cc, hf=hf, kc=kc, p=p,
                               pcol=pcol, first=first, last=last,
                               sg=(sbi > 0): e.matmul(
                            OBANK[p][:, hf * 512:(hf + 1) * 512],
                            PT_t[:, pcol + cc * 128: pcol + (cc + 1) * 128],
                            V_km[:, kc * H + hf * 512: kc * H + (hf + 1) * 512],
                            start=first, stop=last, skip_group_check=sg),
                            reads=[("PT", p, kb), ("V", kc)],
                            writes=[OKEY[p][hf]])

        def p2_fin(i):
            p = i % 2
            xb = i % 2
            P.emit("vector", lambda e, p=p: e.reciprocal(
                rl_t[p][:], l_run[p][:]),
                reads=[("l_run", p)], writes=[("rl", p)], cls="recip")
            P.emit("vector", lambda e, xb=xb, p=p: e.scalar_tensor_tensor(
                x1_t[p][:], OBANK[p][:], rl_t[p][:], x_t[xb][:], ALU.mult,
                ALU.add),
                reads=[OKEY[p][0], OKEY[p][1], ("rl", p), ("x_t", xb)],
                writes=[("x1_t", p)])
            P.emit("sync", lambda e, i=i, p=p: e.dma_start(
                x1s[i * 128:(i + 1) * 128, :], x1_t[p][:]),
                reads=[("x1_t", p)], writes=[("x1s", i)], dma=True)
            if stage == "full":
                def _stats(i=i, p=p):
                    P.emit("scalar", lambda e, i=i, p=p: e.activation(
                        x1_t[p][:], x1_t[p][:], AF.Copy,
                        accum_out=stA["sx"][:, i:i + 1]),
                        reads=[("x1_t", p)],
                        writes=[("x1_t", p), ("st", "sx", i)], cls="accum")
                    P.emit("scalar", lambda e, i=i, p=p: e.activation(
                        x1_t[p][:], x1_t[p][:], AF.Square,
                        accum_out=stA["ssq"][:, i:i + 1]),
                        reads=[("x1_t", p)],
                        writes=[("x1_t", p), ("st", "ssq", i)], cls="accum")
                pending_stats.append(_stats)
            if stage == "p2":
                P.emit("sync", lambda e, i=i, p=p: e.dma_start(
                    y[i * 128:(i + 1) * 128, :], x1_t[p][:]),
                    reads=[("x1_t", p)], dma=True)

        def p2_load_q(pr):
            for i in (2 * pr, 2 * pr + 1):
                p = i % 2
                P.emit("sync", lambda e, i=i, p=p: e.dma_start(
                    q_sb[p][:], qs[i, :, :]),
                    reads=[("qs", i)], writes=[("q_sb", p)], dma=True)

        def p2_load_x(pr):
            for i in (2 * pr, 2 * pr + 1):
                p = i % 2
                P.emit("sync", lambda e, i=i, p=p: e.dma_start(
                    x_t[p][:], xq[i * 128:(i + 1) * 128, :]),
                    writes=[("x_t", p)], dma=True)

        fc1_sb = wkv_sb[:, 8192:12288]
        gate_sb = masks[:, 0:64]
        fc2_sb = wkv_sb[:, 12288:16384]

        pending_stats = []
        for pr in range(8):
            i0, i1 = 2 * pr, 2 * pr + 1
            L = pr + 1
            nsb2 = (L + 1) // 2
            for k in range(nsb2):
                if k == nsb2 - 1 and pr < 7:
                    p2_load_q(pr + 1)
                p2_soft(i0, k)
                p2_ptav(i0, k)
                if k + 1 < nsb2:
                    p2_scores(i0, k + 1)
                elif pr < 7:
                    p2_scores(2 * pr + 2, 0)
                p2_soft(i1, k)
                if k == 0:
                    for f in pending_stats:
                        f()
                    pending_stats.clear()
                p2_ptav(i1, k)
                if k + 1 < nsb2:
                    p2_scores(i1, k + 1)
                elif pr < 7:
                    p2_scores(2 * pr + 3, 0)
            p2_fin(i0)
            p2_fin(i1)
            if pr == 7:
                for f in pending_stats:
                    f()
                pending_stats.clear()
            if pr < 7:
                p2_load_x(pr + 1)
            if pr == 1:
                precast_ud()
            if pr == 6:
                rstd_chain(0, 12)
                load_wT(fc1_sb, fc1Td, 512, "fc1", 8)
                load_wT(fc2_sb, fc2Td, 1024, "fc2", 4)

        w1_sb = V_km[:, 0:16384]
        w2_sb = V_km[:, 16384:32768]
        P.emit("gpsimd", lambda e: e.dma_start(
            gate_sb[:, 0:64], gateTd[:, :]),
            writes=["gate", ("masks", 0)], dma=True)

        def w12_loads():
            gating = ["p3ago"]
            for qtr in range(4):
                P.emit("gpsimd", lambda e, qtr=qtr: e.dma_start(
                    w1_sb[:, qtr * 4096:(qtr + 1) * 4096],
                    w1Td[:, qtr * 4096:(qtr + 1) * 4096]),
                    reads=gating,
                    writes=[("w1", qtr)] + [("V", 4 * qtr + j)
                                            for j in range(4)], dma=True)
                P.emit("gpsimd", lambda e, qtr=qtr: e.dma_start(
                    w2_sb[:, qtr * 4096:(qtr + 1) * 4096],
                    w2Td[:, qtr * 4096:(qtr + 1) * 4096]),
                    reads=gating,
                    writes=[("w2", qtr)] + [("V", 16 + 4 * qtr + j)
                                            for j in range(4)], dma=True)

        if stage == "p2":
            P.wait_all("sync")
            block = es.enter_context(nc.Block())
            realize(P, block, sems)
            return nc

        P.mark("P3a")

        xfT = K_fm[:, 0:8 * 2048]
        latT = K_fm[:, 8 * 2048:8 * 2048 + 4 * 2048]
        comb = K_fm[:, 8 * 2048 + 4 * 2048:8 * 2048 + 8 * 2048]
        h_sh = V_km[:, 8 * 2048:8 * 2048 + 16 * 512 * 2]

        cb_sb = sb("cbt", [128, 128], F32)
        Rs = sb("Rs", [128, 128], F32)
        Rm = sb("Rm", [128, 128], F32)
        Rx = sb("Rx", [128, 128], F32)
        Ry = sb("Ry", [128, 128], F32)
        Rz = sb("Rz", [128, 128], F32)
        Rw = sb("Rw", [128, 128], F32)
        Rq = sb("Rq", [128, 128], F32)
        R64 = sb("R64", [128, 64], F32)
        R32 = sb("R32", [128, 32], F32)
        g16 = {k: sb(f"g16_{k}", [128, 16], F32) for k in
               ("m1a", "n1a", "m2a", "n2a", "Ma", "mma", "nna", "seca", "gA",
                "m1b", "n1b", "m2b", "n2b", "Mb", "mmb", "nnb", "secb", "gB",
                "d", "mA", "mB", "t1", "cnt", "t2c", "two", "d12", "t2",
                "sg", "need", "den", "rden")}
        t1b = [masks[:, 64 + j * 128:64 + (j + 1) * 128] for j in range(2)]
        r_bf = xfTg[0][:, 0:2048]
        rw_bf = [xfTg[0][:, 2048:4096], xfTg[1][:, 2048:4096]]
        r2w_bf = xfTg[1][:, 0:2048]
        r_sh = [xfTg[1][:, 2048 + j * 512:2048 + (j + 1) * 512] for j in range(2)]
        out_sb = x1_t

        P.emit("sync", lambda e: e.dma_start(cb_sb[:], cb_in[:, :]),
               writes=["cb"], dma=True)


        P.mark("P3b")
        ROT4F = [("pA", 0), ("pA", 1), ("pB", 0), ("pB", 1)]
        rotf = [0]
        def p3a_dma(i):
            xb = i % 4
            alias = ("x_t", xb) if xb < 2 else ("x1_t", xb - 2)
            P.emit("sync", lambda e, i=i, xb=xb: e.dma_start(
                xpool[xb][:], x1s[i * 128:(i + 1) * 128, :]),
                reads=[("x1s", i)], writes=[("xp", xb), alias], dma=True)

        def fc1_group(tg, s_evac=False):
            for Lc in range(4):
                bk = ROT4F[rotf[0] % 4]
                rotf[0] += 1
                pt = PSN[bk[0]][:, bk[1] * 512:(bk[1] + 1) * 512]
                for hc in range(8):
                    P.emit("tensor", lambda e, Lc=Lc, tg=tg, hc=hc, pt=pt:
                           e.matmul(
                        pt, fc1_sb[:, hc * 512 + Lc * 128:
                                   hc * 512 + (Lc + 1) * 128],
                        xfT[:, hc * 2048 + tg * 512:
                            hc * 2048 + (tg + 1) * 512],
                        start=(hc == 0), stop=(hc == 7)),
                        reads=[("xfT", tg * 4), ("xfT", tg * 4 + 1),
                               ("xfT", tg * 4 + 2), ("xfT", tg * 4 + 3),
                               "fc1"], writes=[bk])
                dst = latT[:, Lc * 2048 + tg * 512:Lc * 2048 + (tg + 1) * 512]
                if s_evac:
                    P.emit("scalar", lambda e, dst=dst, pt=pt: e.copy(dst, pt),
                           reads=[bk], writes=["lat"])
                else:
                    evac(dst, pt, [bk], ["lat"])

        p3a_dma(0)
        p3a_dma(1)
        for i in range(NT):
            xb = i % 4
            fb = i % 2
            if i + 2 < NT:
                p3a_dma(i + 2)
            if i == 2:
                P.last_write["p3ago"] = P.last_write[("xf_t", 1)]
                w12_loads()
            if i == 3:
                rstd_chain(12, 16)
            normalize2(xb, fb, i, ("xp", xb), ("xf_t", fb))
            for half in range(2):
                bk = ("pC", half)
                for cc in range(4):
                    hc = half * 4 + cc
                    P.emit("tensor", lambda e, hc=hc, half=half, cc=cc, fb=fb:
                           e.matmul(
                        pC[:, half * 512 + cc * 128: half * 512 + (cc + 1) * 128],
                        xf_t[fb][:, hc * 128:(hc + 1) * 128], ident[:],
                        start=True, stop=True),
                        reads=[("xf_t", fb), "ident"], writes=[bk])
            for half in range(2):
                bk = ("pC", half)
                dst = xfT.rearrange("p (hc f) -> p hc f", hc=8)[
                    :, half * 4:(half + 1) * 4, i * 128:(i + 1) * 128]
                srcp = pC[:, half * 512:(half + 1) * 512].rearrange(
                    "p (c f) -> p c f", c=4)
                evac(dst, srcp, [bk], [("xfT", i)])
            gh = i % 2
            for hc in range(8):
                P.emit("tensor", lambda e, hc=hc, i=i, gh=gh: e.matmul(
                    pD[:, gh * 512:gh * 512 + 8],
                    xfT[:, hc * 2048 + i * 128: hc * 2048 + (i + 1) * 128],
                    gate_sb[:, hc * 8:(hc + 1) * 8],
                    start=(hc == 0), stop=(hc == 7)),
                    reads=[("xfT", i), "gate"], writes=[("pD", gh)])
            P.emit("scalar", lambda e, i=i, gh=gh: e.activation(
                Rs[:, i:i + 113:16], pD[:, gh * 512:gh * 512 + 8], AF.Sigmoid),
                reads=[("pD", gh)], writes=["Rs"])
            if i % 4 == 3 and i < 8:
                fc1_group(i // 4)
        fc1_group(2, s_evac=True)
        fc1_group(3, s_evac=True)
        if "dbg_lat" in dbg:
            P.emit("sync", lambda e: e.dma_start(dbg["dbg_lat"][:, :], latT),
                   reads=["lat"], dma=True)

        P.mark("P3c")

        def tt(dst, dk, a, ak, b_, bk_, op):
            P.emit("vector", lambda e: e.tensor_tensor(dst, a, b_, op),
                   reads=list(ak) + list(bk_), writes=[dk], cls="tt")

        def bc8(t16):
            return bass.AP(t16, 0, [[16, 128], [0, 8], [1, 16]])

        r3 = lambda t: t[:].rearrange("p (e c) -> p e c", e=8)
        gk_ = lambda n: ("g", n)

        tt(Rm[:], "Rm", Rs[:], ["Rs"], cb_sb[:], ["cb"], ALU.add)
        for gix, base in (("a", 0), ("b", 64)):
            gg = lambda k: g16[k + gix]
            c0, c1, c2, c3 = (Rm[:, base + 16 * t: base + 16 * (t + 1)]
                              for t in range(4))
            tt(gg("m1")[:], gk_("m1" + gix), c0, ["Rm"], c1, ["Rm"], ALU.max)
            tt(gg("n1")[:], gk_("n1" + gix), c0, ["Rm"], c1, ["Rm"], ALU.min)
            tt(gg("m2")[:], gk_("m2" + gix), c2, ["Rm"], c3, ["Rm"], ALU.max)
            tt(gg("n2")[:], gk_("n2" + gix), c2, ["Rm"], c3, ["Rm"], ALU.min)
            tt(gg("M")[:], gk_("M" + gix), gg("m1")[:], [gk_("m1" + gix)],
               gg("m2")[:], [gk_("m2" + gix)], ALU.max)
            tt(gg("mm")[:], gk_("mm" + gix), gg("m1")[:], [gk_("m1" + gix)],
               gg("m2")[:], [gk_("m2" + gix)], ALU.min)
            tt(gg("nn")[:], gk_("nn" + gix), gg("n1")[:], [gk_("n1" + gix)],
               gg("n2")[:], [gk_("n2" + gix)], ALU.max)
            tt(gg("sec")[:], gk_("sec" + gix), gg("mm")[:], [gk_("mm" + gix)],
               gg("nn")[:], [gk_("nn" + gix)], ALU.max)
            tt(g16["gA" if gix == "a" else "gB"][:], gk_("g" + gix),
               gg("M")[:], [gk_("M" + gix)], gg("sec")[:], [gk_("sec" + gix)],
               ALU.add)
        tt(g16["d"][:], gk_("d"), g16["gA"][:], [gk_("ga")], g16["gB"][:],
           [gk_("gb")], ALU.subtract)
        P.emit("vector", lambda e: e.tensor_single_scalar(
            g16["mA"][:], g16["d"][:], 0.0, ALU.is_ge),
            reads=[gk_("d")], writes=[gk_("mA")])
        P.emit("vector", lambda e: e.tensor_single_scalar(
            g16["mB"][:], g16["d"][:], 0.0, ALU.is_lt),
            reads=[gk_("d")], writes=[gk_("mB")])
        for ee in range(8):
            mk_ = gk_("mA") if ee < 4 else gk_("mB")
            mt = g16["mA"] if ee < 4 else g16["mB"]
            tt(Rm[:, ee * 16:(ee + 1) * 16], "Rm",
               Rm[:, ee * 16:(ee + 1) * 16], ["Rm"], mt[:], [mk_], ALU.mult)
        tt(R64[:], "R64", Rm[:, 0:64], ["Rm"], Rm[:, 64:128], ["Rm"], ALU.max)
        tt(R32[:], "R32", R64[:, 0:32], ["R64"], R64[:, 32:64], ["R64"], ALU.max)
        tt(g16["t1"][:], gk_("t1"), R32[:, 0:16], ["R32"], R32[:, 16:32],
           ["R32"], ALU.max)
        tt(r3(Rx), "Rx", r3(Rm), ["Rm"], bc8(g16["t1"]), [gk_("t1")], ALU.is_equal)
        tt(R64[:], "R64", Rx[:, 0:64], ["Rx"], Rx[:, 64:128], ["Rx"], ALU.add)
        tt(R32[:], "R32", R64[:, 0:32], ["R64"], R64[:, 32:64], ["R64"], ALU.add)
        tt(g16["cnt"][:], gk_("cnt"), R32[:, 0:16], ["R32"], R32[:, 16:32],
           ["R32"], ALU.add)
        P.emit("vector", lambda e: e.tensor_single_scalar(
            Ry[:], Rx[:], -1e30, ALU.mult), reads=["Rx"], writes=["Ry"])
        tt(Ry[:], "Ry", Rm[:], ["Rm"], Ry[:], ["Ry"], ALU.add)
        tt(R64[:], "R64", Ry[:, 0:64], ["Ry"], Ry[:, 64:128], ["Ry"], ALU.max)
        tt(R32[:], "R32", R64[:, 0:32], ["R64"], R64[:, 32:64], ["R64"], ALU.max)
        tt(g16["t2c"][:], gk_("t2c"), R32[:, 0:16], ["R32"], R32[:, 16:32],
           ["R32"], ALU.max)
        P.emit("vector", lambda e: e.tensor_single_scalar(
            g16["two"][:], g16["cnt"][:], 2.0, ALU.is_ge),
            reads=[gk_("cnt")], writes=[gk_("two")])
        tt(g16["d12"][:], gk_("d12"), g16["t1"][:], [gk_("t1")],
           g16["t2c"][:], [gk_("t2c")], ALU.subtract)
        tt(g16["d12"][:], gk_("d12"), g16["d12"][:], [gk_("d12")],
           g16["two"][:], [gk_("two")], ALU.mult)
        tt(g16["t2"][:], gk_("t2"), g16["t2c"][:], [gk_("t2c")],
           g16["d12"][:], [gk_("d12")], ALU.add)
        tt(r3(Rx), "Rx", r3(Rm), ["Rm"], bc8(g16["t2"]), [gk_("t2")], ALU.is_gt)
        tt(R64[:], "R64", Rx[:, 0:64], ["Rx"], Rx[:, 64:128], ["Rx"], ALU.add)
        tt(R32[:], "R32", R64[:, 0:32], ["R64"], R64[:, 32:64], ["R64"], ALU.add)
        tt(g16["sg"][:], gk_("sg"), R32[:, 0:16], ["R32"], R32[:, 16:32],
           ["R32"], ALU.add)
        P.emit("vector", lambda e: e.tensor_scalar(
            g16["need"][:], g16["sg"][:], -1.0, 2.0, ALU.mult, ALU.add),
            reads=[gk_("sg")], writes=[gk_("need")])
        tt(r3(Ry), "Ry", r3(Rm), ["Rm"], bc8(g16["t2"]), [gk_("t2")], ALU.is_equal)
        P.emit("vector", lambda e: e.tensor_copy(Rz[:, 0:16], Ry[:, 0:16]),
               reads=["Ry"], writes=["Rz"])
        tt(Rz[:, 16:128], "Rz", Ry[:, 16:128], ["Ry"], Ry[:, 0:112], ["Ry"],
           ALU.add)
        P.emit("vector", lambda e: e.tensor_copy(Rx[:, 0:32], Rz[:, 0:32]),
               reads=["Rz"], writes=["Rx"])
        tt(Rx[:, 32:128], "Rx", Rz[:, 32:128], ["Rz"], Rz[:, 0:96], ["Rz"],
           ALU.add)
        P.emit("vector", lambda e: e.tensor_copy(Rz[:, 0:64], Rx[:, 0:64]),
               reads=["Rx"], writes=["Rz"])
        tt(Rz[:, 64:128], "Rz", Rx[:, 64:128], ["Rx"], Rx[:, 0:64], ["Rx"],
           ALU.add)
        tt(r3(Rx), "Rx", r3(Rz), ["Rz"], bc8(g16["need"]), [gk_("need")],
           ALU.is_le)
        tt(Ry[:], "Ry", Ry[:], ["Ry"], Rx[:], ["Rx"], ALU.mult)
        tt(r3(Rx), "Rx", r3(Rm), ["Rm"], bc8(g16["t2"]), [gk_("t2")], ALU.is_gt)
        tt(Ry[:], "Ry", Ry[:], ["Ry"], Rx[:], ["Rx"], ALU.add)
        tt(Rx[:], "Rx", Ry[:], ["Ry"], Rs[:], ["Rs"], ALU.mult)
        tt(R64[:], "R64", Rx[:, 0:64], ["Rx"], Rx[:, 64:128], ["Rx"], ALU.add)
        tt(R32[:], "R32", R64[:, 0:32], ["R64"], R64[:, 32:64], ["R64"], ALU.add)
        tt(g16["den"][:], gk_("den"), R32[:, 0:16], ["R32"], R32[:, 16:32],
           ["R32"], ALU.add)
        P.emit("vector", lambda e: e.tensor_scalar(
            g16["den"][:], g16["den"][:], 1.0, 1e-20, ALU.mult, ALU.add),
            reads=[gk_("den")], writes=[gk_("den")])
        P.emit("vector", lambda e: e.reciprocal(g16["rden"][:], g16["den"][:]),
               reads=[gk_("den")], writes=[gk_("rden")], cls="recip")
        tt(r3(Rw), "Rw", r3(Rx), ["Rx"], bc8(g16["rden"]), [gk_("rden")],
           ALU.mult)
        if "dbg_wr" in dbg:
            P.emit("sync", lambda e: e.dma_start(dbg["dbg_wr"][:, :], Rw[:]),
                   reads=["Rw"], dma=True)
        P.emit("scalar", lambda e: e.activation(Rq[:], Rw[:], AF.Sqrt),
               reads=["Rw"], writes=["Wq"])
        wbtb = [wkv_sb[:, 0:4096], wkv_sb[:, 4096:8192]]

        def build_wbt_steps(tg):
            wb = wbtb[tg % 2]
            for ee in range(8):
                def step(ee=ee, wb=wb, tg=tg):
                    for ii in range(4):
                        i = tg * 4 + ii
                        col = ee * 16 + i
                        tb = ii % 2
                        P.emit("vector", lambda e, col=col, tb=tb: e.tensor_copy(
                            t1b[tb], Rq[:, col:col + 1].broadcast_to([128, 128])),
                            reads=["Wq"], writes=[("t1b", tb)])
                        P.emit("tensor", lambda e, tb=tb, ii=ii, tg=tg: e.matmul(
                            pD[:, (tg % 2) * 512 + ii * 128:
                               (tg % 2) * 512 + (ii + 1) * 128],
                            t1b[tb], ident[:], start=True, stop=True),
                            reads=[("t1b", tb), "ident"], writes=[("pD", tg % 2)])
                    evac(wb[:, ee * 512:(ee + 1) * 512],
                         pD[:, (tg % 2) * 512:(tg % 2) * 512 + 512],
                         [("pD", tg % 2)], [("wbt", tg, ee)])
                yield step

        P.mark("P3d")
        CBANKS = (("pA", 0), ("pA", 1), ("pB", 0), ("pB", 1))
        if True:
            pending = {tg: list(build_wbt_steps(tg)) for tg in range(4)}
            for tg in range(4):
                wbt = wbtb[tg % 2]

                ROT4 = [("pC", 0), ("pC", 1)]

                def rw_one(ee, Isub, tg=tg, wbt=wbt):
                    par = ee % 2
                    P.emit("vector", lambda e, Isub=Isub, ee=ee, par=par,
                           wbt=wbt: e.tensor_tensor(
                        rw_bf[par][:, Isub * 512:(Isub + 1) * 512],
                        r_bf[:, Isub * 512:(Isub + 1) * 512],
                        wbt[:, ee * 512:(ee + 1) * 512],
                        ALU.mult),
                        reads=[("r", Isub), ("wbt", tg, ee)],
                        writes=[("rw", par, Isub)], cls="tt")

                def stageA(ee, tg=tg, with_rw=True):
                    par = ee % 2
                    for Isub in range(4):
                        bk = ROT4[(ee * 4 + Isub) % 2]
                        pt = PSN[bk[0]][:, bk[1] * 512:(bk[1] + 1) * 512]
                        for Lc in range(4):
                            P.emit("tensor", lambda e, ee=ee, Isub=Isub, Lc=Lc,
                                   tg=tg, pt=pt: e.matmul(
                                pt,
                                w1_sb[:, (ee * 4 + Lc) * 512 + Isub * 128:
                                      (ee * 4 + Lc) * 512 + (Isub + 1) * 128],
                                latT[:, Lc * 2048 + tg * 512:
                                     Lc * 2048 + (tg + 1) * 512],
                                start=(Lc == 0), stop=(Lc == 3)),
                                reads=["lat", ("w1", ee // 2)], writes=[bk])
                        P.emit("scalar", lambda e, Isub=Isub, pt=pt: e.activation(
                            r_bf[:, Isub * 512:(Isub + 1) * 512], pt, AF.Relu),
                            reads=[bk], writes=[("r", Isub)])
                        if with_rw:
                            rw_one(ee, Isub)

                def stageB(ee):
                    par = ee % 2
                    for Isub in range(4):
                        P.emit("vector", lambda e, Isub=Isub, par=par:
                               e.tensor_tensor(
                            r2w_bf[:, Isub * 512:(Isub + 1) * 512],
                            rw_bf[par][:, Isub * 512:(Isub + 1) * 512],
                            rw_bf[par][:, Isub * 512:(Isub + 1) * 512], ALU.mult),
                            reads=[("rw", par, Isub)], writes=[("r2w", Isub)],
                            cls="tt")
                    for Lsub in range(4):
                        for Isub in range(4):
                            bname, half = CBANKS[Lsub]
                            P.emit("tensor", lambda e, ee=ee, Isub=Isub, Lsub=Lsub,
                                   bname=bname, half=half: e.matmul(
                                PSN[bname][:, half * 512:(half + 1) * 512],
                                w2_sb[:, (ee * 4 + Isub) * 512 + Lsub * 128:
                                      (ee * 4 + Isub) * 512 + (Lsub + 1) * 128],
                                r2w_bf[:, Isub * 512:(Isub + 1) * 512],
                                start=(ee == 0 and Isub == 0),
                                stop=(ee == 7 and Isub == 3)),
                                reads=[("r2w", Isub), ("w2", ee // 2)],
                                writes=[CBANKS[Lsub]])

                nxt = pending.get(tg + 1, [])
                if tg == 0:
                    stageA(0, with_rw=False)
                    for st_ in pending[0]:
                        st_()
                    for Isub in range(4):
                        rw_one(0, Isub)
                else:
                    stageA(0)
                for ee in range(1, 8):
                    stageA(ee)
                    if nxt:
                        nxt.pop(0)()
                    stageB(ee - 1)
                stageB(7)
                if tg == 3:
                    for j in range(2):
                        P.emit("sync", lambda e, j=j: e.dma_start(
                            V_km[:, j * 4096:(j + 1) * 4096], up_bs[j, :, :]),
                            reads=WBSK("u", j),
                            writes=[("up", j), ("w1", j)], dma=True)
                        P.emit("scalar", lambda e, j=j: e.dma_start(
                            K_fm[:, 16384 + j * 4096:16384 + (j + 1) * 4096],
                            dn_bs[j, :, :]),
                            reads=WBSK("d", j),
                            writes=[("dn", j), "lat"], dma=True)
                while nxt:
                    nxt.pop(0)()
                for Lsub in range(4):
                    bname, half = CBANKS[Lsub]
                    evac(comb[:, Lsub * 2048 + tg * 512:
                              Lsub * 2048 + (tg + 1) * 512],
                         PSN[bname][:, half * 512:(half + 1) * 512],
                         [CBANKS[Lsub]], ["comb"])
        if "dbg_comb" in dbg:
            P.emit("sync", lambda e: e.dma_start(dbg["dbg_comb"][:, :], comb),
                   reads=["comb"], dma=True)

        P.mark("P3e")
        P.barrier()
        up_sb = [V_km[:, j * 4096:(j + 1) * 4096] for j in range(2)]
        dn_sb = [K_fm[:, 16384 + j * 4096:16384 + (j + 1) * 4096] for j in range(2)]
        OUTB = ("pA", "pB", "pC", "pD")

        def up_dma(sc):
            ub = sc % 2
            P.emit("sync", lambda e, sc=sc, ub=ub: e.dma_start(
                up_sb[ub][:, 0:4096], up_bs[sc, :, :]),
                reads=WBSK("u", sc), writes=[("up", ub)], dma=True)

        def dn_dma(sc):
            db = sc % 2
            P.emit("sync", lambda e, sc=sc, db=db: e.dma_start(
                dn_sb[db][:, 0:4096], dn_bs[sc, :, :]),
                reads=WBSK("d", sc), writes=[("dn", db)], dma=True)

        for tg in range(4):
            for sc in range(8):
                ub = sc % 2
                for ssub in range(4):
                    bk = (("pC", 0), ("pC", 1), ("pD", 0), ("pD", 1))[ssub]
                    pt = PSN[bk[0]][:, bk[1] * 512:(bk[1] + 1) * 512]
                    for hc in range(8):
                        P.emit("tensor", lambda e, sc=sc, ssub=ssub, hc=hc, ub=ub,
                               tg=tg, pt=pt: e.matmul(
                            pt, up_sb[ub][:, hc * 512 + ssub * 128:
                                          hc * 512 + (ssub + 1) * 128],
                            xfT[:, hc * 2048 + tg * 512: hc * 2048 + (tg + 1) * 512],
                            start=(hc == 0), stop=(hc == 7)),
                            reads=[("up", ub), ("xfT", tg * 4),
                                   ("xfT", tg * 4 + 1), ("xfT", tg * 4 + 2),
                                   ("xfT", tg * 4 + 3)], writes=[bk])
                    rb = ssub % 2
                    P.emit("scalar", lambda e, rb=rb, pt=pt: e.activation(
                        r_sh[rb], pt, AF.Relu),
                        reads=[bk], writes=[("rsh", rb)])
                    P.emit("vector", lambda e, sc=sc, ssub=ssub, rb=rb:
                           e.tensor_tensor(
                        h_sh[:, (sc * 4 + ssub) * 512:(sc * 4 + ssub + 1) * 512],
                        r_sh[rb], r_sh[rb], ALU.mult),
                        reads=[("rsh", rb)], writes=[("hsh", sc * 4 + ssub)],
                        cls="tt")
                if sc + 2 < 8:
                    up_dma(sc + 2)
                if sc < 2 and tg > 0:
                    dn_dma(sc)
            if tg + 1 < 4:
                up_dma(0)
                up_dma(1)
            for xb in range(2):
                ti = tg * 4 + 2 + xb
                P.emit("gpsimd", lambda e, ti=ti, xb=xb: e.dma_start(
                    x_t[xb][:], x1s[ti * 128:(ti + 1) * 128, :]),
                    reads=[("x1s", ti)], writes=[("x_t", xb)], dma=True)
            for sc in range(8):
                db = sc % 2
                for tsub in range(4):
                    for hf in range(2):
                        for ssub in range(4):
                            pn = OUTB[tsub]
                            first = (sc == 0 and ssub == 0)
                            P.emit("tensor", lambda e, sc=sc, ssub=ssub, tsub=tsub,
                                   hf=hf, db=db, pn=pn, first=first: e.matmul(
                                PSN[pn][:, hf * 512:(hf + 1) * 512],
                                h_sh[:, (sc * 4 + ssub) * 512 + tsub * 128:
                                     (sc * 4 + ssub) * 512 + (tsub + 1) * 128],
                                dn_sb[db][:, ssub * 1024 + hf * 512:
                                           ssub * 1024 + (hf + 1) * 512],
                                start=first, stop=False),
                                reads=[("hsh", sc * 4 + ssub), ("dn", db)],
                                writes=[(pn, hf)])
                if sc + 2 < 8:
                    dn_dma(sc + 2)
            for tsub in range(4):
                for hf in range(2):
                    for Lc in range(4):
                        pn = OUTB[tsub]
                        P.emit("tensor", lambda e, tsub=tsub, hf=hf, Lc=Lc, tg=tg,
                               pn=pn: e.matmul(
                            PSN[pn][:, hf * 512:(hf + 1) * 512],
                            comb[:, Lc * 2048 + tg * 512 + tsub * 128:
                                 Lc * 2048 + tg * 512 + (tsub + 1) * 128],
                            fc2_sb[:, Lc * 1024 + hf * 512:
                                   Lc * 1024 + (hf + 1) * 512],
                            start=False, stop=(Lc == 3)),
                            reads=["comb", "fc2"], writes=[(pn, hf)])
            for tsub in (2, 3, 0, 1):
                ti = tg * 4 + tsub
                xb = tsub % 2
                ob = tsub % 2
                pn = OUTB[tsub]
                P.emit("vector", lambda e, pn=pn, xb=xb, ob=ob: e.tensor_tensor(
                    out_sb[ob][:], PSN[pn][:], x_t[xb][:], ALU.add),
                    reads=[(pn, 0), (pn, 1), ("x_t", xb)], writes=[("out", ob)],
                    cls="tt")
                P.emit("gpsimd", lambda e, ti=ti, ob=ob: e.dma_start(
                    y[ti * 128:(ti + 1) * 128, :], out_sb[ob][:]),
                    reads=[("out", ob)], dma=True)
                if tsub >= 2:
                    tr = tg * 4 + (tsub - 2)
                    P.emit("gpsimd", lambda e, tr=tr, xb=xb: e.dma_start(
                        x_t[xb][:], x1s[tr * 128:(tr + 1) * 128, :]),
                        reads=[("x1s", tr)], writes=[("x_t", xb)], dma=True)

        P.wait_all("sync")
        global LAST_MARKS
        LAST_MARKS = list(P.phase_marks)
        block = es.enter_context(nc.Block())
        realize(P, block, sems)
    return nc



def host_inputs(inputs, stage="full", debug=()):
    x = np.asarray(inputs["L0_attn_x"], np.float32)
    t = lambda a: np.ascontiguousarray(np.asarray(a, np.float32).T)

    def tile8(wT, ncols):
        return np.ascontiguousarray(
            wT.reshape(8, 128, ncols).transpose(1, 0, 2).reshape(128, 8 * ncols))

    def tile4(wT, ncols):
        return np.ascontiguousarray(
            wT.reshape(4, 128, ncols).transpose(1, 0, 2).reshape(128, 4 * ncols))

    wqT = tile8(t(inputs["L0_attn_Wq"]), H)
    wkT = tile8(t(inputs["L0_attn_Wk"]), H)
    wvT = tile8(t(inputs["L0_attn_Wv"]), H)
    fc1T = tile8(t(inputs["L0_ffn_fc1_w"]), LAT)
    gateT = tile8(t(inputs["L0_ffn_gate"]), E)
    fc2T = tile4(t(inputs["L0_ffn_fc2_w"]), H)
    upT = np.ascontiguousarray(
        t(inputs["L0_ffn_up_w"]).reshape(8, 128, 8, 512)
        .transpose(2, 1, 0, 3).reshape(8, 128, 4096))
    downT = np.ascontiguousarray(
        t(inputs["L0_ffn_down_w"]).reshape(8, 4, 128, 1024)
        .transpose(0, 2, 1, 3).reshape(8, 128, 4096))
    w1T = np.ascontiguousarray(np.transpose(
        np.asarray(inputs["L0_ffn_w1"], np.float32), (0, 2, 1))
        .reshape(8, 4, 128, 512).transpose(2, 0, 1, 3).reshape(128, 16384))
    w2T = np.ascontiguousarray(np.transpose(
        np.asarray(inputs["L0_ffn_w2"], np.float32), (0, 2, 1))
        .reshape(8, 4, 128, 512).transpose(2, 0, 1, 3).reshape(128, 16384))
    cb = np.asarray(inputs["L0_ffn_corr_bias"], np.float32)
    ident = np.eye(128, dtype=ml_dtypes.bfloat16)
    cb128 = np.ascontiguousarray(
        np.broadcast_to(np.repeat(cb, NT)[None, :], (128, 128))).astype(np.float32)
    pp = np.arange(128)[:, None]
    ccol = np.arange(512)[None, :]
    mask4 = [(ccol <= 128 * j + pp).astype(ml_dtypes.bfloat16) for j in range(4)]
    in_maps = []
    for c in range(8):
        b, d = c // 2, c % 2
        xs = np.ascontiguousarray(x[b])
        xq = np.ascontiguousarray(x[b].reshape(NKT, 128, H)[d::2].reshape(TCORE, H))
        masks = np.stack([mask4[d], mask4[2 + d]])
        qtg = np.arange(NT, dtype=np.int32)[None, :] * 2 + d
        in_maps.append({
            "xseq": xs, "xq": xq, "qtile_g": qtg,
            "wqT": wqT, "wkT": wkT, "wvT": wvT,
            "fc1T": fc1T, "gateT": gateT, "fc2T": fc2T, "upT": upT,
            "downT": downT, "w1T": w1T, "w2T": w2T, "ident": ident,
            "masks": np.ascontiguousarray(masks), "cb128": cb128,
        })
    return in_maps


def kernel(**inputs):
    nc = build_program(stage="full")
    in_maps = host_inputs(inputs)
    res = bass_utils.run_bass_kernel_spmd(nc, in_maps, core_ids=list(range(8)))
    out = np.empty((B, S, H), np.float32)
    for c in range(8):
        b, d = c // 2, c % 2
        yc = np.asarray(res.results[c]["y"], np.float32)
        out[b].reshape(NKT, 128, H)[d::2] = yc.reshape(NT, 128, H)
    return out
